# version 02
# baseline (speedup 1.0000x reference)
import contextlib
import numpy as np
import ml_dtypes

import concourse.bass as bass
import concourse.mybir as mybir
from concourse.bass_utils import run_bass_kernel_spmd

F32 = mybir.dt.float32
BF16 = mybir.dt.bfloat16
AF = mybir.ActivationFunctionType
ALU = mybir.AluOpType
AX = mybir.AxisListType

ENGINES = ("sync", "scalar", "vector", "tensor", "gpsimd")
ENG_SEM = {"scalar": "s_act", "vector": "s_dve", "tensor": "s_pe", "gpsimd": "s_pool"}
NQSEM = 8
DMA_SEMS = {"sync": ["q_sp%d" % i for i in range(NQSEM)],
            "scalar": ["q_act%d" % i for i in range(NQSEM)],
            "gpsimd": ["q_pool%d" % i for i in range(NQSEM)]}


class Buf:
    __slots__ = ("w", "r", "name", "psum")

    def __init__(self, name="", psum=False):
        self.w = None
        self.r = []
        self.name = name
        self.psum = psum


def rebind(name, olds):
    b = Buf(name)
    for o in olds:
        if o.w is not None:
            b.r.append(("x", o.w))
        b.r.extend(o.r)
    return b


class Prog:
    def __init__(self, nc):
        self.nc = nc
        self.q = {e: [] for e in ENGINES}
        allsems = list(ENG_SEM.values())
        for v in DMA_SEMS.values():
            allsems += v
        self.cnt = {s: 0 for s in allsems}
        self.rr = {e: 0 for e in DMA_SEMS}
        self.covered = {}
        self.sems = {}

    def _waits(self, eng, toks):
        need = {}
        for t in toks:
            if t is None:
                continue
            s, v = t
            if need.get(s, 0) < v:
                need[s] = v
        out = []
        for s, v in need.items():
            key = (eng, s)
            if self.covered.get(key, 0) >= v:
                continue
            self.covered[key] = v
            out.append((s, v))
        return out

    def op(self, eng, fn, rd=(), wr=(), warf=(), extra=(), dma=False, n=1):
        toks = []
        for b in rd:
            toks.extend(b.w or ())
            if b.psum:
                for (re, t) in b.r:
                    if re != eng:
                        toks.extend(t)
        for t in extra:
            if t:
                toks.extend(t)
        for b in list(wr) + list(warf):
            toks.extend(b.w or ())
            for (re, t) in b.r:
                toks.extend(t)
        if dma:
            assert n <= NQSEM, "op would wait on its own DMAs"
            sems = []
            for _ in range(n):
                s = DMA_SEMS[eng][self.rr[eng] % NQSEM]
                self.rr[eng] += 1
                toks.append((s, self.cnt[s]))
                self.cnt[s] += 16
                sems.append((s, self.cnt[s]))
            tok = tuple(sems)
        else:
            sem = ENG_SEM[eng]
            self.cnt[sem] += 1
            tok = ((sem, self.cnt[sem]),)
        waits = self._waits(eng, toks)

        def thunk(e, waits=waits, fn=fn, dma=dma, tok=tok, n=n):
            for s, v in waits:
                e.wait_ge(self.sems[s], v)
            res = fn(e)
            if n == 1:
                res.then_inc(self.sems[tok[0][0]], 16 if dma else 1)
            else:
                assert isinstance(res, (list, tuple)) and len(res) == n
                for inst, (s, v) in zip(res, tok):
                    inst.then_inc(self.sems[s], 16)

        self.q[eng].append(thunk)
        for b in rd:
            b.r.append((eng, tok))
        for b in wr:
            b.w = tok
            b.r = []
        return tok

    def build(self):
        nc = self.nc
        with contextlib.ExitStack() as ctx:
            for s in self.cnt:
                self.sems[s] = ctx.enter_context(nc.semaphore(s))
            block = ctx.enter_context(nc.Block())
            final = [(s, v) for s, v in self.cnt.items() if v > 0]

            def mk(engname):
                def body(e):
                    for thunk in self.q[engname]:
                        thunk(e)
                    if engname == "sync":
                        for s, v in final:
                            e.wait_ge(self.sems[s], v)
                return body

            block.sync(mk("sync"))
            block.scalar(mk("scalar"))
            block.vector(mk("vector"))
            block.tensor(mk("tensor"))
            block.gpsimd(mk("gpsimd"))



def host_rotary_tables(S, dh=64, scale=1.0):
    inv = 1.0 / (10000.0 ** (np.arange(0, dh, 2, dtype=np.float64) / dh))
    pos = np.arange(S, dtype=np.float64)
    fr = np.concatenate([np.outer(inv, pos), np.outer(inv, pos)], axis=0)
    cos_t = np.tile(np.cos(fr), (2, 1)) * scale
    sin_t = np.tile(np.sin(fr), (2, 1)) * scale
    return (np.ascontiguousarray(cos_t.astype(np.float32)),
            np.ascontiguousarray(sin_t.astype(np.float32)))


def host_rot_lhsT():
    R = np.zeros((128, 128), dtype=np.float32)
    for blk in range(2):
        o = blk * 64
        for m in range(32):
            R[o + m, o + m + 32] = -1.0
            R[o + m + 32, o + m] = 1.0
    return np.ascontiguousarray(R.T)


def host_headsum():
    hs = np.zeros((128, 16 * 8), dtype=np.float32)
    for oc in range(8):
        hs[0:64, oc * 16 + 2 * oc] = 1.0
        hs[64:128, oc * 16 + 2 * oc + 1] = 1.0
    return hs


def own_tile_ids(S, quad):
    return [4 * k + quad for k in range(S // 128 // 4)]


def to_fm(a):
    return np.ascontiguousarray(a.T)



def host_hot2(QT):
    m = np.zeros((64, 8 * QT * 128), dtype=np.float32)
    for hp in range(8):
        for qt in range(QT):
            o = (hp * QT + qt) * 128
            m[(2 * hp) * QT + qt, o:o + 64] = 1.0
            m[(2 * hp + 1) * QT + qt, o + 64:o + 128] = 1.0
    return m


def host_idrow64():
    m = np.zeros((1, 64 * 64), dtype=np.float32)
    for j in range(64):
        m[0, j * 64 + j] = 1.0
    return m


def build_l0_attn(nc, S, upto='F'):
    H, D = 16, 1024
    NOWN = S // 4
    TT = S // 512
    QT = NOWN // 512
    KC = S // 128
    NG = KC // 16

    x_fm = nc.dram_tensor("x_fm", [D, S], F32, kind="ExternalInput").ap()
    xo_fm = nc.dram_tensor("xo_fm", [D, NOWN], F32, kind="ExternalInput").ap()
    wq_d = nc.dram_tensor("wq", [D, D], F32, kind="ExternalInput").ap()
    wkv_d = nc.dram_tensor("wkv", [D, D], F32, kind="ExternalInput").ap()
    wout_d = nc.dram_tensor("wout", [D, D], F32, kind="ExternalInput").ap()
    bout_d = nc.dram_tensor("bout_t", [128, 8], F32, kind="ExternalInput").ap()
    kcos_d = nc.dram_tensor("kcos", [128, S], F32, kind="ExternalInput").ap()
    ksin_d = nc.dram_tensor("ksin", [128, S], F32, kind="ExternalInput").ap()
    qcos_d = nc.dram_tensor("qcos", [128, NOWN], F32, kind="ExternalInput").ap()
    qsin_d = nc.dram_tensor("qsin", [128, NOWN], F32, kind="ExternalInput").ap()
    rt_d = nc.dram_tensor("rt", [128, 128], F32, kind="ExternalInput").ap()
    hsum_d = nc.dram_tensor("hsum", [128, 16 * 8], F32, kind="ExternalInput").ap()
    id16_d = nc.dram_tensor("id16", [16, 16], F32, kind="ExternalInput").ap()
    hot_d = nc.dram_tensor("hot2", [64, 8 * QT * 128], F32, kind="ExternalInput").ap()
    idm_d = nc.dram_tensor("idmat", [128, 128], F32, kind="ExternalInput").ap()
    idr_d = nc.dram_tensor("idrow64", [1, 64 * 64], F32, kind="ExternalInput").ap()
    y_fm = nc.dram_tensor("y_fm", [D, NOWN], F32, kind="ExternalOutput").ap()

    kvf_d = nc.dram_tensor("kvf", [D, S], BF16, kind="Internal").ap()
    kvt_d = nc.dram_tensor("kvt", [S, H * 65], BF16, kind="Internal").ap()

    with contextlib.ExitStack() as es:
        p = Prog(nc)
        E = es.enter_context

        AB = E(nc.sbuf_tensor("AB", [128, 17664], BF16))
        KB = E(nc.sbuf_tensor("KB", [128, 16384], BF16))
        XF = E(nc.sbuf_tensor("XF", [128, 4096], F32))
        qp_s = E(nc.sbuf_tensor("qp_s", [128, 8 * NOWN], BF16))
        oacc_s = E(nc.sbuf_tensor("oacc_s", [128, 8 * NOWN], BF16))
        scrA_s = E(nc.sbuf_tensor("scrA_s", [128, 512], F32))
        scrB_s = E(nc.sbuf_tensor("scrB_s", [128, 512], F32))
        scrC_s = E(nc.sbuf_tensor("scrC_s", [128, 512], F32))
        kcs_s = E(nc.sbuf_tensor("kcs_s", [128, 512], F32))
        ksn_s = E(nc.sbuf_tensor("ksn_s", [128, 512], F32))
        qcs_s = E(nc.sbuf_tensor("qcs_s", [128, 512], F32))
        qsn_s = E(nc.sbuf_tensor("qsn_s", [128, 512], F32))
        st_s = E(nc.sbuf_tensor("st_s", [1, 1024], F32))
        rstdB_s = E(nc.sbuf_tensor("rstdB_s", [128, 512], F32))
        rcpB_s = E(nc.sbuf_tensor("rcpB_s", [128, 512], F32))
        srow_s = E(nc.sbuf_tensor("srow_s", [1, 512], F32))
        sums_s = E(nc.sbuf_tensor("sums_s", [64, 512], F32))
        kn2_s = E(nc.sbuf_tensor("kn2_s", [16, 2], F32))
        nred_s = E(nc.sbuf_tensor("nred_s", [16, 1], F32))
        mrow_s = E(nc.sbuf_tensor("mrow_s", [1, 16], F32))
        mneg_s = E(nc.sbuf_tensor("mneg_s", [128, 1], F32))
        bout_s = E(nc.sbuf_tensor("bout_s", [128, 8], F32))
        rt_s = E(nc.sbuf_tensor("rt_s", [128, 128], BF16))
        idm_s = E(nc.sbuf_tensor("idm_s", [128, 128], BF16))
        hsum_s = E(nc.sbuf_tensor("hsum_s", [128, 16 * 8], F32))
        id16_s = E(nc.sbuf_tensor("id16_s", [16, 16], F32))
        hot_s = E(nc.sbuf_tensor("hot_s", [64, 8 * QT * 128], F32))
        idr_s = E(nc.sbuf_tensor("idr_s", [1, 64 * 64], F32))
        ones_c = E(nc.sbuf_tensor("ones_c", [128, 1], F32))
        ones_r = E(nc.sbuf_tensor("ones_r", [1, 128], F32))
        eps5_s = E(nc.sbuf_tensor("eps5_s", [128, 1], F32))
        eps30_s = E(nc.sbuf_tensor("eps30_s", [128, 1], F32))

        ps_t, ps_b = [], []
        for i in range(7):
            ps_t.append(E(nc.psum_tensor(f"ps{i}", [128, 512], F32)))
            ps_b.append(Buf(f"ps{i}", psum=True))

        wq_b, wkv_b = Buf("wq"), Buf("wkv")
        xln_b, kvp_b, stg_b, kv0_b = Buf("xln"), Buf("kvp"), Buf("stg"), Buf("kv0")
        xt_b = Buf("xt")
        scrA_b, scrB_b, scrC_b = Buf("scrA"), Buf("scrB"), Buf("scrC")
        kcs_b, ksn_b = Buf("kcs"), Buf("ksn")
        qcs_b, qsn_b = Buf("qcs"), Buf("qsn")
        st_b, rstdB_b, rcpB_b = Buf("st"), Buf("rstdB"), Buf("rcpB")
        srow_b, sums_b = Buf("srow"), Buf("sums")
        kn2_b, nred_b, mrow_b, mneg_b = Buf("kn2"), Buf("nred"), Buf("mrow"), Buf("mneg")
        bout_b, rt_b, hsum_b, id16_b = Buf("bout"), Buf("rt"), Buf("hsum"), Buf("id16")
        idm_b = Buf("idm")
        hot_b, idr_b = Buf("hot"), Buf("idr")
        ones_c_b, ones_r_b = Buf("ones_c"), Buf("ones_r")
        eps5_b, eps30_b = Buf("eps5"), Buf("eps30")
        qp_b = Buf("qp")
        oacc_b = [Buf(f"oacc{i}") for i in range(8 * QT)]

        wq_v = AB[:, 0:8192]
        wkv_v = AB[:, 8192:16384]
        xln_v = KB[:, 0:4096]
        kvp_v = KB[:, 4096:8192]
        stg_v = KB[:, 8192:12352]
        kv0_v = KB[:, 12352:12864]
        xt_v = XF[:, 0:4096]
        qp_r = qp_s.rearrange("p (f t) -> p f t", f=8)
        oacc_r = oacc_s.rearrange("p (f t) -> p f t", f=8)
        stg_r = stg_v.rearrange("p (t h x) -> p t h x", t=4, h=H)
        xln_r = xln_v.rearrange("p (f t) -> p f t", f=8)
        kvp_r = kvp_v.rearrange("p (f t) -> p f t", f=8)
        xt_r = xt_v.rearrange("p (f t) -> p f t", f=8)

        def wload(dst_v, src_d, b):
            def fn(e):
                return [e.dma_start(out=dst_v[:, kc * 1024:(kc + 1) * 1024],
                                    in_=src_d[kc * 128:(kc + 1) * 128, :])
                        for kc in range(8)]
            p.op("gpsimd", fn, wr=[b], dma=True, n=8)

        wload(wq_v, wq_d, wq_b)
        wload(wkv_v, wkv_d, wkv_b)
        p.op("gpsimd", lambda e: e.dma_start(out=rt_s[:], in_=rt_d), wr=[rt_b], dma=True)
        p.op("gpsimd", lambda e: e.dma_start(out=idm_s[:], in_=idm_d), wr=[idm_b], dma=True)
        p.op("sync", lambda e: e.dma_start(out=bout_s[:], in_=bout_d), wr=[bout_b], dma=True)
        p.op("sync", lambda e: e.dma_start(out=hsum_s[:], in_=hsum_d), wr=[hsum_b], dma=True)
        p.op("sync", lambda e: e.dma_start(out=id16_s[:], in_=id16_d), wr=[id16_b], dma=True)
        p.op("sync", lambda e: e.dma_start(out=hot_s[:], in_=hot_d), wr=[hot_b], dma=True)
        p.op("sync", lambda e: e.dma_start(out=idr_s[:], in_=idr_d), wr=[idr_b], dma=True)
        p.op("vector", lambda e: e.memset(ones_c[:], 1.0), wr=[ones_c_b])
        p.op("vector", lambda e: e.memset(eps5_s[:], 1e-5), wr=[eps5_b])
        p.op("vector", lambda e: e.memset(eps30_s[:], 1e-30), wr=[eps30_b])
        p.op("vector", lambda e: e.memset(ones_r[:], 1.0), wr=[ones_r_b])
        p.op("vector", lambda e: e.memset(kn2_s[:], -1.0), wr=[kn2_b])
        p.op("vector", lambda e: e.memset(stg_r[:, :, :, 64:65], 1.0), wr=[stg_b])

        def sq_pingpong(src_chunk_ap_fn, src_b, ps_idx, lhsT_fn, lhsT_b, M):
            scr = (scrB_s, scrC_s)
            scrb = (scrB_b, scrC_b)
            for fc in range(8):
                p.op("scalar", lambda e, fc=fc: e.activation(
                    scr[fc % 2][:], src_chunk_ap_fn(fc), AF.Square),
                    rd=[src_b], wr=[scrb[fc % 2]])
                p.op("tensor", lambda e, fc=fc, M=M: e.matmul(
                    ps_t[ps_idx][0:M, :], lhsT_fn(fc), scr[fc % 2][:],
                    start=(fc == 0), stop=(fc == 7)),
                    rd=[scrb[fc % 2], lhsT_b],
                    wr=[ps_b[ps_idx]] if fc == 7 else [],
                    warf=[ps_b[ps_idx]] if fc == 0 else [])

        def ln_tile(src_r, src_b, dst_r, dst_b):
            mu = st_s[0:1, 0:512]
            tmp = st_s[0:1, 512:1024]

            def mk_sum(e):
                for fc in range(8):
                    mm = e.matmul(ps_t[0][0:1, :], ones_c[:], src_r[:, fc, :],
                                  start=(fc == 0), stop=(fc == 7))
                return mm
            p.op("tensor", mk_sum, rd=[src_b, ones_c_b], wr=[ps_b[0]])
            sq_pingpong(lambda fc: src_r[:, fc, :], src_b, 1,
                        lambda fc: ones_c[:], ones_c_b, 1)
            p.op("scalar", lambda e: e.activation(mu, ps_t[0][0:1, :], AF.Copy,
                                                  scale=1.0 / 1024),
                 rd=[ps_b[0]], wr=[st_b])
            p.op("scalar", lambda e: e.activation(tmp, mu, AF.Square),
                 rd=[st_b], wr=[st_b])
            p.op("vector", lambda e: e.scalar_tensor_tensor(
                tmp, ps_t[1][0:1, :], 1.0 / 1024, tmp, ALU.mult, ALU.subtract),
                rd=[ps_b[1], st_b], wr=[st_b])
            p.op("scalar", lambda e: e.activation(tmp, tmp, AF.Ln,
                                                  bias=eps5_s[0:1, :]),
                 rd=[st_b, eps5_b], wr=[st_b])
            p.op("scalar", lambda e: e.activation(tmp, tmp, AF.Exp, scale=-0.5),
                 rd=[st_b], wr=[st_b])
            p.op("tensor", lambda e: e.matmul(ps_t[3][:], ones_r[:], mu,
                                              start=True, stop=True),
                 rd=[st_b, ones_r_b], wr=[ps_b[3]])
            p.op("tensor", lambda e: e.matmul(ps_t[6][:], ones_r[:], tmp,
                                              start=True, stop=True),
                 rd=[st_b, ones_r_b], wr=[ps_b[6]])
            p.op("scalar", lambda e: e.activation(rstdB_s[:], ps_t[6][:], AF.Copy),
                 rd=[ps_b[6]], wr=[rstdB_b])
            for fc in range(8):
                p.op("vector", lambda e, fc=fc: e.tensor_sub(
                    scrA_s[:], src_r[:, fc, :], ps_t[3][:]),
                    rd=[src_b, ps_b[3]], wr=[scrA_b])
                p.op("vector", lambda e, fc=fc: e.tensor_mul(
                    dst_r[:, fc, :], scrA_s[:], rstdB_s[:]),
                    rd=[scrA_b, rstdB_b],
                    wr=[dst_b] if fc == 7 else [], warf=[dst_b] if fc == 0 else [])

        def proj_rot(w_v, w_b, src_r, src_b, oc, cosap, sinap, cb, sb,
                     dst_ap, dst_b, dst_first, dst_last):
            def mk(e):
                for kc in range(8):
                    mm = e.matmul(ps_t[4][:],
                                  w_v[:, kc * 1024 + oc * 128: kc * 1024 + (oc + 1) * 128],
                                  src_r[:, kc, :], start=(kc == 0), stop=(kc == 7))
                return mm
            p.op("tensor", mk, rd=[w_b, src_b], wr=[ps_b[4]])
            p.op("scalar", lambda e: e.activation(kv0_v[:], ps_t[4][:], AF.Copy),
                 rd=[ps_b[4]], wr=[kv0_b])
            p.op("tensor", lambda e: e.matmul(ps_t[5][:], rt_s[:], kv0_v[:],
                                              start=True, stop=True),
                 rd=[rt_b, kv0_b], wr=[ps_b[5]])
            p.op("vector", lambda e: e.tensor_mul(scrA_s[:], kv0_v[:], cosap),
                 rd=[kv0_b, cb], wr=[scrA_b])
            p.op("vector", lambda e: e.tensor_mul(scrB_s[:], ps_t[5][:], sinap),
                 rd=[ps_b[5], sb], wr=[scrB_b])
            p.op("vector", lambda e: e.tensor_add(dst_ap, scrA_s[:], scrB_s[:]),
                 rd=[scrA_b, scrB_b],
                 wr=[dst_b] if dst_last else [], warf=[dst_b] if dst_first else [])

        def fold_norms(col):
            p.op("vector", lambda e: e.tensor_reduce(nred_s[:], ps_t[6][0:16, :],
                                                     AX.X, ALU.max),
                 rd=[ps_b[6]], wr=[nred_b])
            p.op("vector", lambda e, col=col: e.tensor_tensor(
                kn2_s[0:16, col:col + 1], kn2_s[0:16, col:col + 1], nred_s[:],
                ALU.max), rd=[nred_b], wr=[kn2_b])

        for tt in range(TT):
            ts = tt * 512

            def mk_xld(e, ts=ts):
                return [e.dma_start(out=xt_r[:, fc, :],
                                    in_=x_fm[fc * 128:(fc + 1) * 128, ts:ts + 512])
                        for fc in range(8)]
            p.op("sync", mk_xld, wr=[xt_b], dma=True, n=8)
            p.op("sync", lambda e, ts=ts: e.dma_start(
                out=kcs_s[:], in_=kcos_d[:, ts:ts + 512]), wr=[kcs_b], dma=True)
            p.op("sync", lambda e, ts=ts: e.dma_start(
                out=ksn_s[:], in_=ksin_d[:, ts:ts + 512]), wr=[ksn_b], dma=True)

            ln_tile(xt_r, xt_b, xln_r, xln_b)
            for oc in range(8):
                proj_rot(wkv_v, wkv_b, xln_r, xln_b, oc, kcs_s[:], ksn_s[:],
                         kcs_b, ksn_b, kvp_r[:, oc, :], kvp_b, oc == 0, oc == 7)
            sq_pingpong(lambda fc: kvp_r[:, fc, :], kvp_b, 6,
                        lambda fc: hsum_s[:, fc * 16:(fc + 1) * 16], hsum_b, 16)
            fold_norms(0)

            def mk_kvst(e, ts=ts):
                return [e.dma_start(out=kvf_d[oc * 128:(oc + 1) * 128, ts:ts + 512],
                                    in_=kvp_r[:, oc, :]) for oc in range(8)]
            p.op("sync", mk_kvst, rd=[kvp_b], dma=True, n=8)

            for tb in range(4):
                for oc in range(8):
                    p.op("tensor", lambda e, tb=tb, oc=oc: e.matmul(
                        ps_t[5][:, 0:128], kvp_r[:, oc, tb * 128:(tb + 1) * 128],
                        idm_s[:], start=True, stop=True),
                        rd=[kvp_b, idm_b], wr=[ps_b[5]])
                    p.op("scalar", lambda e, tb=tb, oc=oc: e.activation(
                        stg_r[:, tb, 2 * oc:2 * oc + 2, 0:64],
                        ps_t[5][:, 0:128].rearrange("p (h x) -> p h x", h=2),
                        AF.Copy), rd=[ps_b[5]], wr=[stg_b])
                p.op("sync", lambda e, tb=tb, ts=ts: e.dma_start(
                    out=kvt_d[ts + tb * 128: ts + (tb + 1) * 128, :],
                    in_=stg_v[:, tb * H * 65:(tb + 1) * H * 65]), rd=[stg_b], dma=True)

        def finish_early(e_=None):
            def mkcp(e):
                return [e.dma_start(out=y_fm[i * 128:(i + 1) * 128, :],
                                    in_=xo_fm[i * 128:(i + 1) * 128, :])
                        for i in range(8)]
            p.op("sync", mkcp, dma=True, n=8)
            p.build()
            return True

        if upto == 'A':
            return finish_early() and nc

        for qt in range(QT):
            def mk_xld2(e, qt=qt):
                return [e.dma_start(out=xt_r[:, fc, :],
                                    in_=xo_fm[fc * 128:(fc + 1) * 128,
                                              qt * 512:(qt + 1) * 512])
                        for fc in range(8)]
            p.op("sync", mk_xld2, wr=[xt_b], dma=True, n=8)
            p.op("sync", lambda e, qt=qt: e.dma_start(
                out=qcs_s[:], in_=qcos_d[:, qt * 512:(qt + 1) * 512]),
                wr=[qcs_b], dma=True)
            p.op("sync", lambda e, qt=qt: e.dma_start(
                out=qsn_s[:], in_=qsin_d[:, qt * 512:(qt + 1) * 512]),
                wr=[qsn_b], dma=True)
            ln_tile(xt_r, xt_b, xln_r, xln_b)
            for oc in range(8):
                proj_rot(wq_v, wq_b, xln_r, xln_b, oc, qcs_s[:], qsn_s[:],
                         qcs_b, qsn_b,
                         qp_r[:, oc, qt * 512:(qt + 1) * 512], qp_b,
                         qt == 0 and oc == 0, qt == QT - 1 and oc == 7)
            sq_pingpong(lambda fc, qt=qt: qp_r[:, fc, qt * 512:(qt + 1) * 512],
                        qp_b, 6,
                        lambda fc: hsum_s[:, fc * 16:(fc + 1) * 16], hsum_b, 16)
            fold_norms(1)

        if upto == 'C':
            return finish_early() and nc

        p.op("vector", lambda e: e.tensor_mul(nred_s[:], kn2_s[0:16, 0:1],
                                              kn2_s[0:16, 1:2]), rd=[kn2_b], wr=[nred_b])
        p.op("scalar", lambda e: e.activation(nred_s[:], nred_s[:], AF.Sqrt),
             rd=[nred_b], wr=[nred_b])
        p.op("tensor", lambda e: e.matmul(ps_t[4][0:1, 0:16], nred_s[:], id16_s[:],
                                          start=True, stop=True),
             rd=[nred_b, id16_b], wr=[ps_b[4]])
        p.op("scalar", lambda e: e.activation(mrow_s[:], ps_t[4][0:1, 0:16], AF.Copy),
             rd=[ps_b[4]], wr=[mrow_b])
        p.op("vector", lambda e: e.tensor_reduce(mrow_s[0:1, 0:1], mrow_s[:],
                                                 AX.X, ALU.max), rd=[mrow_b], wr=[mrow_b])
        p.op("tensor", lambda e: e.matmul(ps_t[4][:, 0:1], ones_r[:],
                                          mrow_s[0:1, 0:1], start=True, stop=True),
             rd=[mrow_b, ones_r_b], wr=[ps_b[4]])
        p.op("vector", lambda e: e.tensor_scalar(mneg_s[:], ps_t[4][:, 0:1], -1.0,
                                                 None, ALU.mult), rd=[ps_b[4]], wr=[mneg_b])

        if upto == 'D':
            return finish_early() and nc

        vslab_v = AB[:, 0:16640]
        pt_v = AB[:, 16640:17664]
        kslab_v = KB[:, 0:16384]
        vslab_b = rebind("vslab", [wq_b, wkv_b])
        pt_b = [Buf("pt0"), Buf("pt1")]
        kslab_b = rebind("kslab", [xln_b, kvp_b, stg_b, kv0_b])
        kslab_r = kslab_v.rearrange("p (f t) -> p f t", f=8)
        vslab_r = vslab_v.rearrange("p (k h x) -> p k h x", k=16, h=H)
        pt_r = pt_v.rearrange("p (i t) -> p i t", i=2)
        sumacc_b = Buf("sumacc")

        n_av = 0
        for g in range(NG):
            def mk_kld(e, g=g):
                return [e.dma_start(out=kslab_r[:, fc, :],
                                    in_=kvf_d[fc * 128:(fc + 1) * 128,
                                              g * 2048:(g + 1) * 2048])
                        for fc in range(8)]
            p.op("sync", mk_kld, wr=[kslab_b], dma=True, n=8)

            for half in range(2):
                def mk_vld(e, g=g, half=half):
                    return [e.dma_start(out=vslab_v[:, kc * H * 65:(kc + 1) * H * 65],
                                        in_=kvt_d[(g * 16 + kc) * 128:
                                                  (g * 16 + kc + 1) * 128, :])
                            for kc in range(half * 8, half * 8 + 8)]
                p.op("sync", mk_vld, wr=[vslab_b], dma=True, n=8)

            for qt in range(QT):
                for h in range(H):
                    fcK, row = h // 2, (h % 2) * 64

                    def mm_S(e, kc, qt=qt, fcK=fcK, row=row):
                        return e.matmul(
                            ps_t[kc % 2][:],
                            kslab_r[row:row + 64, fcK, kc * 128:(kc + 1) * 128],
                            qp_r[row:row + 64, fcK, qt * 512:(qt + 1) * 512],
                            start=True, stop=True)
                    p.op("tensor", lambda e, mm_S=mm_S: mm_S(e, 0),
                         rd=[kslab_b, qp_b], wr=[ps_b[0]])
                    for kc in range(16):
                        b = kc % 2
                        if kc + 1 < 16:
                            p.op("tensor", lambda e, kc=kc, mm_S=mm_S: mm_S(e, kc + 1),
                                 rd=[kslab_b, qp_b], wr=[ps_b[(kc + 1) % 2]])
                        p.op("scalar", lambda e, b=b: e.activation(
                            pt_r[:, b, :], ps_t[b][:], AF.Exp, bias=mneg_s[:]),
                            rd=[ps_b[b], mneg_b], wr=[pt_b[b]])
                        p.op("tensor", lambda e, kc=kc, h=h, b=b: e.matmul(
                            ps_t[2][0:65, :], vslab_r[:, kc, h, :], pt_r[:, b, :],
                            start=(kc == 0), stop=(kc == 15)),
                            rd=[vslab_b, pt_b[b]],
                            wr=[ps_b[2]] if kc == 15 else [],
                            warf=[ps_b[2]] if kc == 0 else [])
                    ob = oacc_b[(h // 2) * QT + qt]
                    hrow = (h % 2) * 64
                    if g == 0:
                        p.op("vector", lambda e, h=h, qt=qt, hrow=hrow: e.tensor_copy(
                            oacc_r[hrow:hrow + 64, h // 2, qt * 512:(qt + 1) * 512],
                            ps_t[2][0:64, :]), rd=[ps_b[2]], wr=[ob])
                    else:
                        p.op("vector", lambda e, h=h, qt=qt, hrow=hrow: e.tensor_add(
                            oacc_r[hrow:hrow + 64, h // 2, qt * 512:(qt + 1) * 512],
                            oacc_r[hrow:hrow + 64, h // 2, qt * 512:(qt + 1) * 512],
                            ps_t[2][0:64, :]), rd=[ps_b[2]], wr=[ob])
                    p.op("vector", lambda e: e.tensor_copy(srow_s[:], ps_t[2][64:65, :]),
                         rd=[ps_b[2]], wr=[srow_b])
                    hq = h * QT + qt
                    first_av = (n_av == 0)
                    last_av = (n_av == NG * QT * H - 1)
                    p.op("tensor", lambda e, hq=hq, first_av=first_av, last_av=last_av:
                         e.matmul(ps_t[3][0:64, :], idr_s[0:1, hq * 64:(hq + 1) * 64],
                                  srow_s[:], start=first_av, stop=last_av),
                         rd=[srow_b, idr_b],
                         wr=[ps_b[3]] if last_av else [],
                         warf=[ps_b[3]] if first_av else [])
                    n_av += 1

        if upto == 'E':
            return finish_early() and nc

        p.op("scalar", lambda e: e.activation(sums_s[:], ps_t[3][0:64, :], AF.Ln,
                                              bias=eps30_s[0:64, :]),
             rd=[ps_b[3], eps30_b], wr=[sums_b])
        p.op("scalar", lambda e: e.activation(sums_s[:], sums_s[:], AF.Exp,
                                              scale=-1.0), rd=[sums_b], wr=[sums_b])
        for qt in range(QT):
            for hp in range(8):
                o = (hp * QT + qt) * 128
                p.op("tensor", lambda e, o=o: e.matmul(
                    ps_t[5][:], hot_s[:, o:o + 128], sums_s[:],
                    start=True, stop=True), rd=[sums_b, hot_b], wr=[ps_b[5]])
                p.op("scalar", lambda e: e.activation(rcpB_s[:], ps_t[5][:],
                                                      AF.Copy), rd=[ps_b[5]], wr=[rcpB_b])
                ob = oacc_b[hp * QT + qt]
                p.op("vector", lambda e, hp=hp, qt=qt: e.tensor_mul(
                    oacc_r[:, hp, qt * 512:(qt + 1) * 512],
                    oacc_r[:, hp, qt * 512:(qt + 1) * 512],
                    rcpB_s[:]), rd=[rcpB_b], wr=[ob])

        wout_v = AB[:, 0:8192]
        wout_b = rebind("wout", [vslab_b])
        wload(wout_v, wout_d, wout_b)
        xo_v = XF[:, 0:512]
        yo_v = XF[:, 512:1024]
        xo_b = rebind("xo", [xt_b])
        yo_b = rebind("yo", [xt_b])
        for qt in range(QT):
            for oc in range(8):
                p.op("sync", lambda e, oc=oc, qt=qt: e.dma_start(
                    out=xo_v[:], in_=xo_fm[oc * 128:(oc + 1) * 128,
                                           qt * 512:(qt + 1) * 512]),
                    wr=[xo_b], dma=True)

                def mk_wo(e, oc=oc, qt=qt):
                    for kc in range(8):
                        mm = e.matmul(ps_t[4][:],
                                      wout_v[:, kc * 1024 + oc * 128: kc * 1024 + (oc + 1) * 128],
                                      oacc_r[:, kc, qt * 512:(qt + 1) * 512],
                                      start=(kc == 0), stop=(kc == 7))
                    return mm
                p.op("tensor", mk_wo,
                     rd=[wout_b] + [oacc_b[hp * QT + qt] for hp in range(8)],
                     wr=[ps_b[4]])
                p.op("scalar", lambda e, oc=oc: e.activation(
                    scrA_s[:], ps_t[4][:], AF.Identity, bias=bout_s[:, oc:oc + 1]),
                    rd=[ps_b[4], bout_b], wr=[scrA_b])
                p.op("vector", lambda e: e.tensor_add(yo_v[:], scrA_s[:], xo_v[:]),
                     rd=[scrA_b, xo_b], wr=[yo_b])
                p.op("sync", lambda e, oc=oc, qt=qt: e.dma_start(
                    out=y_fm[oc * 128:(oc + 1) * 128, qt * 512:(qt + 1) * 512],
                    in_=yo_v[:]), rd=[yo_b], dma=True)

        p.build()
    return nc


def prep_l0_attn_inmaps(x, Wq, Wkv, Wout, bout, S):
    xf = [to_fm(np.asarray(x[b], dtype=np.float32)) for b in range(2)]
    cos_t, sin_t = host_rotary_tables(S, 64, 1.0)
    qcos_t, qsin_t = host_rotary_tables(S, 64, 64 ** -0.5)
    bout_t = np.ascontiguousarray(np.asarray(bout, np.float32).reshape(8, 128).T)
    hs = host_headsum()
    rt = host_rot_lhsT()
    id16 = np.eye(16, dtype=np.float32)
    hot = host_hot2(S // 4 // 512)
    idr = host_idrow64()
    maps = []
    for c in range(8):
        b, quad = c // 4, c % 4
        cols = np.concatenate([np.arange(t * 128, (t + 1) * 128)
                               for t in own_tile_ids(S, quad)])
        maps.append(dict(
            x_fm=xf[b], xo_fm=np.ascontiguousarray(xf[b][:, cols]),
            wq=np.asarray(Wq, np.float32), wkv=np.asarray(Wkv, np.float32),
            wout=np.asarray(Wout, np.float32), bout_t=bout_t,
            kcos=cos_t, ksin=sin_t,
            qcos=np.ascontiguousarray(qcos_t[:, cols]),
            qsin=np.ascontiguousarray(qsin_t[:, cols]),
            rt=rt, hsum=hs, id16=id16, hot2=hot, idrow64=idr,
            idmat=np.eye(128, dtype=np.float32),
        ))
    return maps


def gather_own(res_list, S, key="y_fm"):
    D = 1024
    out = np.empty((2, S, D), dtype=np.float32)
    for c in range(8):
        b, quad = c // 4, c % 4
        y = np.asarray(res_list[c][key])
        for k, t in enumerate(own_tile_ids(S, quad)):
            out[b, t * 128:(t + 1) * 128, :] = y[:, k * 128:(k + 1) * 128].T
    return out





def build_l1_attn(nc, S, upto='F'):
    H, D = 16, 1024
    NOWN = S // 4
    TT = S // 512
    QT = NOWN // 512
    KC = S // 128
    NG = KC // 16
    SCALE = 0.125

    x_fm = nc.dram_tensor("x_fm", [D, S], F32, kind="ExternalInput").ap()
    xo_fm = nc.dram_tensor("xo_fm", [D, NOWN], F32, kind="ExternalInput").ap()
    wq_d = nc.dram_tensor("wq", [D, D], F32, kind="ExternalInput").ap()
    wk_d = nc.dram_tensor("wk", [D, D], F32, kind="ExternalInput").ap()
    wv_d = nc.dram_tensor("wv", [D, D], F32, kind="ExternalInput").ap()
    wo_d = nc.dram_tensor("wo", [D, D], F32, kind="ExternalInput").ap()
    hsum_d = nc.dram_tensor("hsum", [128, 16 * 8], F32, kind="ExternalInput").ap()
    id16_d = nc.dram_tensor("id16", [16, 16], F32, kind="ExternalInput").ap()
    hot_d = nc.dram_tensor("hot2", [64, 8 * QT * 128], F32, kind="ExternalInput").ap()
    idr_d = nc.dram_tensor("idrow64", [1, 64 * 64], F32, kind="ExternalInput").ap()
    cmask_d = nc.dram_tensor("cmask", [128, 16 * 512], BF16, kind="ExternalInput").ap()
    y_fm = nc.dram_tensor("y_fm", [D, NOWN], F32, kind="ExternalOutput").ap()

    kf_d = nc.dram_tensor("kf", [D, S], BF16, kind="Internal").ap()
    vt_d = nc.dram_tensor("vt", [S, H * 65], BF16, kind="Internal").ap()

    with contextlib.ExitStack() as es:
        p = Prog(nc)
        E = es.enter_context

        AB = E(nc.sbuf_tensor("AB", [128, 17664], BF16))
        KB = E(nc.sbuf_tensor("KB", [128, 20544], BF16))
        XF = E(nc.sbuf_tensor("XF", [128, 4096], F32))
        qp_s = E(nc.sbuf_tensor("qp_s", [128, 8 * NOWN], BF16))
        oacc_s = E(nc.sbuf_tensor("oacc_s", [128, 8 * NOWN], BF16))
        cmask_s = E(nc.sbuf_tensor("cmask_s", [128, 16 * 512], BF16))
        scrA_s = E(nc.sbuf_tensor("scrA_s", [128, 512], F32))
        scrB_s = E(nc.sbuf_tensor("scrB_s", [128, 512], F32))
        scrC_s = E(nc.sbuf_tensor("scrC_s", [128, 512], F32))
        st_s = E(nc.sbuf_tensor("st_s", [1, 1024], F32))
        rstdB_s = E(nc.sbuf_tensor("rstdB_s", [128, 512], F32))
        rcpB_s = E(nc.sbuf_tensor("rcpB_s", [128, 512], F32))
        srow_s = E(nc.sbuf_tensor("srow_s", [1, 512], F32))
        sums_s = E(nc.sbuf_tensor("sums_s", [64, 512], F32))
        kn2_s = E(nc.sbuf_tensor("kn2_s", [16, 2], F32))
        nred_s = E(nc.sbuf_tensor("nred_s", [16, 1], F32))
        mrow_s = E(nc.sbuf_tensor("mrow_s", [1, 16], F32))
        mneg_s = E(nc.sbuf_tensor("mneg_s", [128, 1], F32))
        hsum_s = E(nc.sbuf_tensor("hsum_s", [128, 16 * 8], F32))
        id16_s = E(nc.sbuf_tensor("id16_s", [16, 16], F32))
        hot_s = E(nc.sbuf_tensor("hot_s", [64, max(8 * QT * 128, 4096)], F32))
        ones_c = E(nc.sbuf_tensor("ones_c", [128, 1], F32))
        ones_r = E(nc.sbuf_tensor("ones_r", [1, 128], F32))
        eps5_s = E(nc.sbuf_tensor("eps5_s", [128, 1], F32))
        eps30_s = E(nc.sbuf_tensor("eps30_s", [128, 1], F32))

        ps_t, ps_b = [], []
        for i in range(7):
            ps_t.append(E(nc.psum_tensor(f"ps{i}", [128, 512], F32)))
            ps_b.append(Buf(f"ps{i}", psum=True))

        wq_b, wk_b, wv_b = Buf("wq"), Buf("wk"), Buf("wv")
        xln_b, kvp_b, stg_b = Buf("xln"), Buf("kvp"), Buf("stg")
        xt_b = Buf("xt")
        scrA_b, scrB_b, scrC_b = Buf("scrA"), Buf("scrB"), Buf("scrC")
        st_b, rstdB_b, rcpB_b = Buf("st"), Buf("rstdB"), Buf("rcpB")
        srow_b, sums_b = Buf("srow"), Buf("sums")
        kn2_b, nred_b, mrow_b, mneg_b = Buf("kn2"), Buf("nred"), Buf("mrow"), Buf("mneg")
        hsum_b, id16_b = Buf("hsum"), Buf("id16")
        hot_b, cmask_b = Buf("hot"), Buf("cmask")
        idr_b = Buf("idr")
        ones_c_b, ones_r_b = Buf("ones_c"), Buf("ones_r")
        eps5_b, eps30_b = Buf("eps5"), Buf("eps30")
        qp_b = Buf("qp")
        oacc_b = [Buf(f"oacc{i}") for i in range(8 * QT)]

        wq_v = AB[:, 0:8192]
        wk_v = AB[:, 8192:16384]
        wv_v = KB[:, 12352:20544]
        xln_v = KB[:, 0:4096]
        kvp_v = KB[:, 4096:8192]
        stg_v = KB[:, 8192:12352]
        xt_v = XF[:, 0:4096]
        qp_r = qp_s.rearrange("p (f t) -> p f t", f=8)
        oacc_r = oacc_s.rearrange("p (f t) -> p f t", f=8)
        stg_r = stg_v.rearrange("p (t h x) -> p t h x", t=4, h=H)
        xln_r = xln_v.rearrange("p (f t) -> p f t", f=8)
        kvp_r = kvp_v.rearrange("p (f t) -> p f t", f=8)
        xt_r = xt_v.rearrange("p (f t) -> p f t", f=8)
        cmask_r = cmask_s.rearrange("p (k t) -> p k t", k=16)

        def wload(dst_v, src_d, b):
            def fn(e):
                return [e.dma_start(out=dst_v[:, kc * 1024:(kc + 1) * 1024],
                                    in_=src_d[kc * 128:(kc + 1) * 128, :])
                        for kc in range(8)]
            p.op("gpsimd", fn, wr=[b], dma=True, n=8)

        wload(wq_v, wq_d, wq_b)
        wload(wk_v, wk_d, wk_b)
        wload(wv_v, wv_d, wv_b)
        p.op("gpsimd", lambda e: e.dma_start(out=cmask_s[:], in_=cmask_d),
             wr=[cmask_b], dma=True)
        p.op("sync", lambda e: e.dma_start(out=hsum_s[:], in_=hsum_d), wr=[hsum_b], dma=True)
        p.op("sync", lambda e: e.dma_start(out=id16_s[:], in_=id16_d), wr=[id16_b], dma=True)
        p.op("sync", lambda e: e.dma_start(out=hot_s[0:1, :], in_=idr_d),
             wr=[idr_b], dma=True)
        p.op("vector", lambda e: e.memset(ones_c[:], 1.0), wr=[ones_c_b])
        p.op("vector", lambda e: e.memset(eps5_s[:], 1e-5), wr=[eps5_b])
        p.op("vector", lambda e: e.memset(eps30_s[:], 1e-30), wr=[eps30_b])
        p.op("vector", lambda e: e.memset(ones_r[:], 1.0), wr=[ones_r_b])
        p.op("vector", lambda e: e.memset(kn2_s[:], -1.0), wr=[kn2_b])
        p.op("vector", lambda e: e.memset(stg_r[:, :, :, 64:65], 1.0), wr=[stg_b])

        def sq_pingpong(src_chunk_ap_fn, src_b, ps_idx, lhsT_fn, lhsT_b, M):
            scr = (scrB_s, scrC_s)
            scrb = (scrB_b, scrC_b)
            for fc in range(8):
                p.op("scalar", lambda e, fc=fc: e.activation(
                    scr[fc % 2][:], src_chunk_ap_fn(fc), AF.Square),
                    rd=[src_b], wr=[scrb[fc % 2]])
                p.op("tensor", lambda e, fc=fc, M=M: e.matmul(
                    ps_t[ps_idx][0:M, :], lhsT_fn(fc), scr[fc % 2][:],
                    start=(fc == 0), stop=(fc == 7)),
                    rd=[scrb[fc % 2], lhsT_b],
                    wr=[ps_b[ps_idx]] if fc == 7 else [],
                    warf=[ps_b[ps_idx]] if fc == 0 else [])

        def ln_tile(src_r, src_b, dst_r_fn, dst_b, dfirst, dlast):
            mu = st_s[0:1, 0:512]
            tm = st_s[0:1, 512:1024]

            def mk_sum(e):
                for fc in range(8):
                    mm = e.matmul(ps_t[0][0:1, :], ones_c[:], src_r(fc),
                                  start=(fc == 0), stop=(fc == 7))
                return mm
            p.op("tensor", mk_sum, rd=[src_b, ones_c_b], wr=[ps_b[0]])
            sq_pingpong(src_r, src_b, 1, lambda fc: ones_c[:], ones_c_b, 1)
            p.op("scalar", lambda e: e.activation(mu, ps_t[0][0:1, :], AF.Copy,
                                                  scale=1.0 / 1024),
                 rd=[ps_b[0]], wr=[st_b])
            p.op("scalar", lambda e: e.activation(tm, mu, AF.Square),
                 rd=[st_b], wr=[st_b])
            p.op("vector", lambda e: e.scalar_tensor_tensor(
                tm, ps_t[1][0:1, :], 1.0 / 1024, tm, ALU.mult, ALU.subtract),
                rd=[ps_b[1], st_b], wr=[st_b])
            p.op("scalar", lambda e: e.activation(tm, tm, AF.Ln, bias=eps5_s[0:1, :]),
                 rd=[st_b, eps5_b], wr=[st_b])
            p.op("scalar", lambda e: e.activation(tm, tm, AF.Exp, scale=-0.5),
                 rd=[st_b], wr=[st_b])
            p.op("tensor", lambda e: e.matmul(ps_t[3][:], ones_r[:], mu,
                                              start=True, stop=True),
                 rd=[st_b, ones_r_b], wr=[ps_b[3]])
            p.op("tensor", lambda e: e.matmul(ps_t[6][:], ones_r[:], tm,
                                              start=True, stop=True),
                 rd=[st_b, ones_r_b], wr=[ps_b[6]])
            p.op("scalar", lambda e: e.activation(rstdB_s[:], ps_t[6][:], AF.Copy),
                 rd=[ps_b[6]], wr=[rstdB_b])
            for fc in range(8):
                p.op("vector", lambda e, fc=fc: e.tensor_sub(
                    scrA_s[:], src_r(fc), ps_t[3][:]),
                    rd=[src_b, ps_b[3]], wr=[scrA_b])
                p.op("vector", lambda e, fc=fc: e.tensor_mul(
                    dst_r_fn(fc), scrA_s[:], rstdB_s[:]),
                    rd=[scrA_b, rstdB_b],
                    wr=[dst_b] if (dlast and fc == 7) else [],
                    warf=[dst_b] if (dfirst and fc == 0) else [])

        for tt in range(TT):
            ts = tt * 512

            def mk_xld(e, ts=ts):
                return [e.dma_start(out=xt_r[:, fc, :],
                                    in_=x_fm[fc * 128:(fc + 1) * 128, ts:ts + 512])
                        for fc in range(8)]
            p.op("sync", mk_xld, wr=[xt_b], dma=True, n=8)
            ln_tile(lambda fc: xt_r[:, fc, :], xt_b,
                    lambda fc: xln_r[:, fc, :], xln_b, True, True)

            for oc in range(8):
                def mk(e, oc=oc):
                    for kc in range(8):
                        mm = e.matmul(ps_t[4][:],
                                      wk_v[:, kc * 1024 + oc * 128: kc * 1024 + (oc + 1) * 128],
                                      xln_r[:, kc, :], start=(kc == 0), stop=(kc == 7))
                    return mm
                p.op("tensor", mk, rd=[wk_b, xln_b], wr=[ps_b[4]])
                p.op("scalar", lambda e, oc=oc: e.activation(
                    kvp_r[:, oc, :], ps_t[4][:], AF.Copy), rd=[ps_b[4]],
                    wr=[kvp_b] if oc == 7 else [], warf=[kvp_b] if oc == 0 else [])
            sq_pingpong(lambda fc: kvp_r[:, fc, :], kvp_b, 6,
                        lambda fc: hsum_s[:, fc * 16:(fc + 1) * 16], hsum_b, 16)
            p.op("vector", lambda e: e.tensor_reduce(nred_s[:], ps_t[6][0:16, :],
                                                     AX.X, ALU.max),
                 rd=[ps_b[6]], wr=[nred_b])
            p.op("vector", lambda e: e.tensor_tensor(
                kn2_s[0:16, 0:1], kn2_s[0:16, 0:1], nred_s[:], ALU.max),
                rd=[nred_b], wr=[kn2_b])

            def mk_kst(e, ts=ts):
                return [e.dma_start(out=kf_d[oc * 128:(oc + 1) * 128, ts:ts + 512],
                                    in_=kvp_r[:, oc, :]) for oc in range(8)]
            p.op("sync", mk_kst, rd=[kvp_b], dma=True, n=8)

            for tb in range(4):
                for fh in range(2):
                    def mkv(e, tb=tb, fh=fh):
                        for kc in range(8):
                            mm = e.matmul(
                                ps_t[5][:],
                                xln_r[:, kc, tb * 128:(tb + 1) * 128],
                                wv_v[:, kc * 1024 + fh * 512: kc * 1024 + (fh + 1) * 512],
                                start=(kc == 0), stop=(kc == 7))
                        return mm
                    p.op("tensor", mkv, rd=[wv_b, xln_b], wr=[ps_b[5]])
                    p.op("scalar", lambda e, tb=tb, fh=fh: e.activation(
                        stg_r[:, tb, fh * 8:(fh + 1) * 8, 0:64],
                        ps_t[5].rearrange("p (h x) -> p h x", h=8), AF.Copy),
                        rd=[ps_b[5]], wr=[stg_b])
                p.op("sync", lambda e, tb=tb, ts=ts: e.dma_start(
                    out=vt_d[ts + tb * 128: ts + (tb + 1) * 128, :],
                    in_=stg_v[:, tb * H * 65:(tb + 1) * H * 65]), rd=[stg_b], dma=True)

        def finish_early():
            def mkcp(e):
                return [e.dma_start(out=y_fm[i * 128:(i + 1) * 128, :],
                                    in_=xo_fm[i * 128:(i + 1) * 128, :])
                        for i in range(8)]
            p.op("sync", mkcp, dma=True, n=8)
            p.build()
            return True

        if upto == 'A':
            return finish_early() and nc

        for qt in range(QT):
            def mk_xld2(e, qt=qt):
                return [e.dma_start(out=xt_r[:, fc, :],
                                    in_=xo_fm[fc * 128:(fc + 1) * 128,
                                              qt * 512:(qt + 1) * 512])
                        for fc in range(8)]
            p.op("sync", mk_xld2, wr=[xt_b], dma=True, n=8)
            ln_tile(lambda fc: xt_r[:, fc, :], xt_b,
                    lambda fc: xln_r[:, fc, :], xln_b, True, True)
            for oc in range(8):
                def mkq(e, qt=qt, oc=oc):
                    for kc in range(8):
                        mm = e.matmul(ps_t[4][:],
                                      wq_v[:, kc * 1024 + oc * 128: kc * 1024 + (oc + 1) * 128],
                                      xln_r[:, kc, :], start=(kc == 0), stop=(kc == 7))
                    return mm
                p.op("tensor", mkq, rd=[wq_b, xln_b], wr=[ps_b[4]])
                p.op("scalar", lambda e, qt=qt, oc=oc: e.activation(
                    qp_r[:, oc, qt * 512:(qt + 1) * 512], ps_t[4][:], AF.Copy,
                    scale=SCALE), rd=[ps_b[4]],
                    wr=[qp_b] if (qt == QT - 1 and oc == 7) else [],
                    warf=[qp_b] if (qt == 0 and oc == 0) else [])
            sq_pingpong(lambda fc, qt=qt: qp_r[:, fc, qt * 512:(qt + 1) * 512],
                        qp_b, 6,
                        lambda fc: hsum_s[:, fc * 16:(fc + 1) * 16], hsum_b, 16)
            p.op("vector", lambda e: e.tensor_reduce(nred_s[:], ps_t[6][0:16, :],
                                                     AX.X, ALU.max),
                 rd=[ps_b[6]], wr=[nred_b])
            p.op("vector", lambda e: e.tensor_tensor(
                kn2_s[0:16, 1:2], kn2_s[0:16, 1:2], nred_s[:], ALU.max),
                rd=[nred_b], wr=[kn2_b])

        if upto == 'C':
            return finish_early() and nc

        p.op("vector", lambda e: e.tensor_mul(nred_s[:], kn2_s[0:16, 0:1],
                                              kn2_s[0:16, 1:2]), rd=[kn2_b], wr=[nred_b])
        p.op("scalar", lambda e: e.activation(nred_s[:], nred_s[:], AF.Sqrt),
             rd=[nred_b], wr=[nred_b])
        p.op("tensor", lambda e: e.matmul(ps_t[4][0:1, 0:16], nred_s[:], id16_s[:],
                                          start=True, stop=True),
             rd=[nred_b, id16_b], wr=[ps_b[4]])
        p.op("scalar", lambda e: e.activation(mrow_s[:], ps_t[4][0:1, 0:16], AF.Copy),
             rd=[ps_b[4]], wr=[mrow_b])
        p.op("vector", lambda e: e.tensor_reduce(mrow_s[0:1, 0:1], mrow_s[:],
                                                 AX.X, ALU.max), rd=[mrow_b], wr=[mrow_b])
        p.op("tensor", lambda e: e.matmul(ps_t[4][:, 0:1], ones_r[:],
                                          mrow_s[0:1, 0:1], start=True, stop=True),
             rd=[mrow_b, ones_r_b], wr=[ps_b[4]])
        p.op("vector", lambda e: e.tensor_scalar(mneg_s[:], ps_t[4][:, 0:1], -1.0,
                                                 None, ALU.mult), rd=[ps_b[4]], wr=[mneg_b])

        if upto == 'D':
            return finish_early() and nc

        vslab_v = AB[:, 0:16640]
        pt_v = AB[:, 16640:17664]
        kslab_v = KB[:, 0:16384]
        vslab_b = rebind("vslab", [wq_b, wk_b])
        pt_b = [Buf("pt0"), Buf("pt1")]
        kslab_b = rebind("kslab", [xln_b, kvp_b, stg_b, wv_b])
        kslab_r = kslab_v.rearrange("p (f t) -> p f t", f=8)
        vslab_r = vslab_v.rearrange("p (k h x) -> p k h x", k=16, h=H)
        pt_r = pt_v.rearrange("p (i t) -> p i t", i=2)

        n_scat = 0
        N_SCAT = H * sum(qt + 1 for qt in range(QT))
        for g in range(NG):
            def mk_kld(e, g=g):
                return [e.dma_start(out=kslab_r[:, fc, :],
                                    in_=kf_d[fc * 128:(fc + 1) * 128,
                                             g * 2048:(g + 1) * 2048])
                        for fc in range(8)]
            p.op("sync", mk_kld, wr=[kslab_b], dma=True, n=8)
            for half in range(2):
                def mk_vld(e, g=g, half=half):
                    return [e.dma_start(out=vslab_v[:, kc * H * 65:(kc + 1) * H * 65],
                                        in_=vt_d[(g * 16 + kc) * 128:
                                                 (g * 16 + kc + 1) * 128, :])
                            for kc in range(half * 8, half * 8 + 8)]
                p.op("sync", mk_vld, wr=[vslab_b], dma=True, n=8)

            for qt in range(g, QT):
                bdry = (g == qt)
                for h in range(H):
                    fcK, row = h // 2, (h % 2) * 64

                    def mm_S(e, kc, qt=qt, fcK=fcK, row=row):
                        return e.matmul(
                            ps_t[kc % 2][:],
                            kslab_r[row:row + 64, fcK, kc * 128:(kc + 1) * 128],
                            qp_r[row:row + 64, fcK, qt * 512:(qt + 1) * 512],
                            start=True, stop=True)
                    p.op("tensor", lambda e, mm_S=mm_S: mm_S(e, 0),
                         rd=[kslab_b, qp_b], wr=[ps_b[0]])
                    for kc in range(16):
                        b = kc % 2
                        if kc + 1 < 16:
                            p.op("tensor", lambda e, kc=kc, mm_S=mm_S: mm_S(e, kc + 1),
                                 rd=[kslab_b, qp_b], wr=[ps_b[(kc + 1) % 2]])
                        p.op("scalar", lambda e, b=b: e.activation(
                            pt_r[:, b, :], ps_t[b][:], AF.Exp, bias=mneg_s[:]),
                            rd=[ps_b[b], mneg_b], wr=[pt_b[b]])
                        if bdry:
                            p.op("vector", lambda e, b=b, kc=kc: e.tensor_mul(
                                pt_r[:, b, :], pt_r[:, b, :], cmask_r[:, kc, :]),
                                rd=[cmask_b], wr=[pt_b[b]])
                        p.op("tensor", lambda e, kc=kc, h=h, b=b, g=g: e.matmul(
                            ps_t[2][0:65, :], vslab_r[:, kc, h, :], pt_r[:, b, :],
                            start=(kc == 0), stop=(kc == 15)),
                            rd=[vslab_b, pt_b[b]],
                            wr=[ps_b[2]] if kc == 15 else [],
                            warf=[ps_b[2]] if kc == 0 else [])
                    ob = oacc_b[(h // 2) * QT + qt]
                    hrow = (h % 2) * 64
                    if g == 0:
                        p.op("vector", lambda e, h=h, qt=qt, hrow=hrow: e.tensor_copy(
                            oacc_r[hrow:hrow + 64, h // 2, qt * 512:(qt + 1) * 512],
                            ps_t[2][0:64, :]), rd=[ps_b[2]], wr=[ob])
                    else:
                        p.op("vector", lambda e, h=h, qt=qt, hrow=hrow: e.tensor_add(
                            oacc_r[hrow:hrow + 64, h // 2, qt * 512:(qt + 1) * 512],
                            oacc_r[hrow:hrow + 64, h // 2, qt * 512:(qt + 1) * 512],
                            ps_t[2][0:64, :]), rd=[ps_b[2]], wr=[ob])
                    p.op("vector", lambda e: e.tensor_copy(srow_s[:], ps_t[2][64:65, :]),
                         rd=[ps_b[2]], wr=[srow_b])
                    hq = h * QT + qt
                    p.op("tensor", lambda e, hq=hq, n_scat=n_scat:
                         e.matmul(ps_t[3][0:64, :], hot_s[0:1, hq * 64:(hq + 1) * 64],
                                  srow_s[:], start=(n_scat == 0),
                                  stop=(n_scat == N_SCAT - 1)),
                         rd=[srow_b, idr_b],
                         wr=[ps_b[3]] if n_scat == N_SCAT - 1 else [],
                         warf=[ps_b[3]] if n_scat == 0 else [])
                    n_scat += 1

        if upto == 'E':
            return finish_early() and nc

        hot_b = rebind("hot2", [idr_b, hot_b])
        p.op("sync", lambda e: e.dma_start(out=hot_s[:, 0:8 * QT * 128], in_=hot_d),
             wr=[hot_b], dma=True)
        p.op("scalar", lambda e: e.activation(sums_s[:], ps_t[3][0:64, :], AF.Ln,
                                              bias=eps30_s[0:64, :]),
             rd=[ps_b[3], eps30_b], wr=[sums_b])
        p.op("scalar", lambda e: e.activation(sums_s[:], sums_s[:], AF.Exp,
                                              scale=-1.0), rd=[sums_b], wr=[sums_b])
        for qt in range(QT):
            for hp in range(8):
                o = (hp * QT + qt) * 128
                p.op("tensor", lambda e, o=o: e.matmul(
                    ps_t[5][:, 0:512], hot_s[:, o:o + 128], sums_s[:],
                    start=True, stop=True), rd=[sums_b, hot_b], wr=[ps_b[5]])
                p.op("scalar", lambda e: e.activation(rcpB_s[:], ps_t[5][:],
                                                      AF.Copy), rd=[ps_b[5]], wr=[rcpB_b])
                ob = oacc_b[hp * QT + qt]
                p.op("vector", lambda e, hp=hp, qt=qt: e.tensor_mul(
                    oacc_r[:, hp, qt * 512:(qt + 1) * 512],
                    oacc_r[:, hp, qt * 512:(qt + 1) * 512],
                    rcpB_s[:]), rd=[rcpB_b], wr=[ob])

        wo_v = AB[:, 0:8192]
        wo_b = rebind("wo", [vslab_b])
        wload(wo_v, wo_d, wo_b)
        xo_v = XF[:, 0:512]
        yo_v = XF[:, 512:1024]
        xo_b = rebind("xo", [xt_b])
        yo_b = rebind("yo", [xt_b])
        for qt in range(QT):
            for oc in range(8):
                p.op("sync", lambda e, oc=oc, qt=qt: e.dma_start(
                    out=xo_v[:], in_=xo_fm[oc * 128:(oc + 1) * 128,
                                           qt * 512:(qt + 1) * 512]),
                    wr=[xo_b], dma=True)

                def mk_wo(e, oc=oc, qt=qt):
                    for kc in range(8):
                        mm = e.matmul(ps_t[4][:],
                                      wo_v[:, kc * 1024 + oc * 128: kc * 1024 + (oc + 1) * 128],
                                      oacc_r[:, kc, qt * 512:(qt + 1) * 512],
                                      start=(kc == 0), stop=(kc == 7))
                    return mm
                p.op("tensor", mk_wo,
                     rd=[wo_b] + [oacc_b[hp * QT + qt] for hp in range(8)],
                     wr=[ps_b[4]])
                p.op("scalar", lambda e: e.activation(scrA_s[:], ps_t[4][:], AF.Copy),
                     rd=[ps_b[4]], wr=[scrA_b])
                p.op("vector", lambda e: e.tensor_add(yo_v[:], scrA_s[:], xo_v[:]),
                     rd=[scrA_b, xo_b], wr=[yo_b])
                p.op("sync", lambda e, oc=oc, qt=qt: e.dma_start(
                    out=y_fm[oc * 128:(oc + 1) * 128, qt * 512:(qt + 1) * 512],
                    in_=yo_v[:]), rd=[yo_b], dma=True)

        p.build()
    return nc


def host_cmask(quad):
    m = np.zeros((128, 16 * 512), dtype=np.float32)
    for kc in range(16):
        for jj in range(4):
            gq = 4 * jj + quad
            for ql in range(128):
                kl_max = gq * 128 + ql - kc * 128
                if kl_max >= 127:
                    m[:, kc * 512 + jj * 128 + ql] = 1.0
                elif kl_max >= 0:
                    m[0:kl_max + 1, kc * 512 + jj * 128 + ql] = 1.0
    return np.ascontiguousarray(m.astype(ml_dtypes.bfloat16))


def prep_l1_attn_inmaps(x_fm_batches, xo_fm_list, Wq, Wk, Wv, Wo, S):
    hs = host_headsum()
    id16 = np.eye(16, dtype=np.float32)
    hot = host_hot2(S // 4 // 512)
    idr = host_idrow64()
    maps = []
    for c in range(8):
        b, quad = c // 4, c % 4
        maps.append(dict(
            x_fm=x_fm_batches[b], xo_fm=xo_fm_list[c],
            wq=np.asarray(Wq, np.float32), wk=np.asarray(Wk, np.float32),
            wv=np.asarray(Wv, np.float32), wo=np.asarray(Wo, np.float32),
            hsum=hs, id16=id16, hot2=hot, idrow64=idr,
            cmask=host_cmask(quad),
        ))
    return maps



def build_l0_ffn(nc, NT, stop_after=None):
    D = 1024
    TL = NT // 512

    x_fm = nc.dram_tensor("x_fm", [D, NT], F32, kind="ExternalInput").ap()
    y_fm = nc.dram_tensor("y_fm", [D, NT], F32, kind="ExternalOutput").ap()
    wnames = ["router_w", "sw_w1", "sw_w3", "sw_w2", "me_eq_w", "me_wv", "me_wo",
              "me_c1w", "me_c2w", "ce_syn_w", "ce_wv", "ce_wo", "ce_f1w",
              "ce_f2w", "ce_gen_w", "ml_w1", "ml_w2"]
    wshape = {"router_w": (D, 4), "sw_w1": (D, 4096), "sw_w3": (D, 4096),
              "sw_w2": (4096, D), "me_c1w": (D, 2048), "me_c2w": (2048, D),
              "ce_f1w": (D, 2048), "ce_f2w": (2048, D), "ml_w1": (D, 4096),
              "ml_w2": (4096, D)}
    wd = {n: nc.dram_tensor(n, list(wshape.get(n, (D, D))), F32,
                            kind="ExternalInput").ap() for n in wnames}
    bnames = ["me_eq_b", "me_bv", "me_bo", "me_c1b", "me_c2b", "ce_syn_b",
              "ce_bv", "ce_bo", "ce_n1g", "ce_n1b", "ce_f1b", "ce_f2b",
              "ce_n2g", "ce_n2b", "ce_gen_b", "ml_b1", "ml_b2"]
    bcols = {"me_c1b": 16, "ce_f1b": 16, "ml_b1": 32}
    bd = {n: nc.dram_tensor(n, [128, bcols.get(n, 8)], F32,
                            kind="ExternalInput").ap() for n in bnames}
    rb_d = nc.dram_tensor("router_b", [4, 1], F32, kind="ExternalInput").ap()
    lb_d = nc.dram_tensor("load_balancer", [4, 1], F32, kind="ExternalInput").ap()
    idm_d = nc.dram_tensor("idmat", [128, 128], F32, kind="ExternalInput").ap()

    with contextlib.ExitStack() as es:
        p = Prog(nc)
        E = es.enter_context

        WHALF = 16384
        WA = E(nc.sbuf_tensor("WA", [128, 2 * WHALF], BF16))
        SA = E(nc.sbuf_tensor("SA", [128, 32768], BF16))
        xl_s = E(nc.sbuf_tensor("xl_s", [128, 8 * NT], BF16))
        xt_s = E(nc.sbuf_tensor("xt_s", [128, 4096], F32))
        gfA_s = E(nc.sbuf_tensor("gfA_s", [64, NT], F32))
        gfB_s = E(nc.sbuf_tensor("gfB_s", [64, NT], F32))
        ones64 = E(nc.sbuf_tensor("ones64", [64, 128], F32))
        gbt_s = xt_s[:, 0:512]
        tmp_s = xt_s[:, 512:1536]
        scrA_s = E(nc.sbuf_tensor("scrA_s", [128, 512], F32))
        scrB_s = E(nc.sbuf_tensor("scrB_s", [128, 512], F32))
        scrC_s = E(nc.sbuf_tensor("scrC_s", [128, 512], F32))
        st_s = E(nc.sbuf_tensor("st_s", [1, 1024], F32))
        rstdB_s = E(nc.sbuf_tensor("rstdB_s", [128, 512], F32))
        rw32_s = E(nc.sbuf_tensor("rw32_s", [128, 32], F32))
        lt_s = E(nc.sbuf_tensor("lt_s", [128, 32], F32))
        gtm_s = E(nc.sbuf_tensor("gtm_s", [128, 128], F32))
        rbl_s = E(nc.sbuf_tensor("rbl_s", [4, 2], F32))
        idf_s = E(nc.sbuf_tensor("idf_s", [128, 128], F32))
        id4_s = E(nc.sbuf_tensor("id4_s", [4, 4], F32))
        ones_c = E(nc.sbuf_tensor("ones_c", [128, 1], F32))
        ones_cb = E(nc.sbuf_tensor("ones_cb", [128, 1], BF16))
        ones_r = E(nc.sbuf_tensor("ones_r", [1, 128], F32))
        eps5_s = E(nc.sbuf_tensor("eps5_s", [128, 1], F32))
        bt = {n: E(nc.sbuf_tensor("bt_" + n, [128, bcols.get(n, 8)], F32))
              for n in bnames}

        ps_t, ps_b = [], []
        for i in range(7):
            ps_t.append(E(nc.psum_tensor(f"ps{i}", [128, 512], F32)))
            ps_b.append(Buf(f"ps{i}", psum=True))

        WAb = [Buf("WA0"), Buf("WA1")]
        xl_b, xt_b = Buf("xl"), Buf("xt")
        gf_b, gbt_b, ones64_b = Buf("gf"), Buf("gbt"), Buf("ones64")
        tmp_b = [Buf("tmp0"), Buf("tmp1")]
        scrA_b, scrB_b, scrC_b = Buf("scrA"), Buf("scrB"), Buf("scrC")
        st_b, rstdB_b = Buf("st"), Buf("rstdB")
        lt_b, gtm_b, rbl_b = Buf("lt"), Buf("gtm"), Buf("rbl")
        rw32_b = Buf("rw32")
        idf_b, id4_b = Buf("idf"), Buf("id4")
        ones_c_b, ones_r_b, eps5_b = Buf("ones_c"), Buf("ones_r"), Buf("eps5")
        btb = {n: Buf("b_" + n) for n in bnames}
        y_b = [Buf(f"y{i}") for i in range(8 * TL)]
        SAb = [Buf("SA0"), Buf("SA1")]

        xl_r = xl_s.rearrange("p (f t) -> p f t", f=8)
        xt_r = xt_s.rearrange("p (f t) -> p f t", f=8)

        def sten(reg):
            def f(kc, t):
                off = reg * 16384
                return SA[:, off + kc * NT + t * 512: off + kc * NT + (t + 1) * 512]
            return f

        def sten2(kc, t):
            return SA[:, kc * NT + t * 512: kc * NT + (t + 1) * 512]

        p.op("sync", lambda e: e.dma_start(out=idf_s[:], in_=idm_d),
             wr=[idf_b], dma=True)
        p.op("sync", lambda e: e.dma_start(out=id4_s[:], in_=idm_d[0:4, 0:4]),
             wr=[id4_b], dma=True)
        p.op("sync", lambda e: e.dma_start(out=rbl_s[:, 0:1], in_=rb_d),
             wr=[rbl_b], dma=True)
        p.op("sync", lambda e: e.dma_start(out=rbl_s[:, 1:2], in_=lb_d),
             wr=[rbl_b], dma=True)
        for n in bnames:
            p.op("sync", lambda e, n=n: e.dma_start(out=bt[n][:], in_=bd[n]),
                 wr=[btb[n]], dma=True)
        p.op("vector", lambda e: e.memset(ones_c[:], 1.0), wr=[ones_c_b])
        p.op("vector", lambda e: e.memset(ones_cb[:], 1.0), wr=[ones_c_b])
        p.op("vector", lambda e: e.memset(ones_r[:], 1.0), wr=[ones_r_b])
        p.op("vector", lambda e: e.memset(ones64[:], 1.0), wr=[ones64_b])
        p.op("vector", lambda e: e.memset(eps5_s[:], 1e-5), wr=[eps5_b])
        p.op("vector", lambda e: e.memset(gtm_s[:], 0.0), wr=[gtm_b])
        p.op("vector", lambda e: e.tensor_add(rbl_s[:, 0:1], rbl_s[:, 0:1],
                                              rbl_s[:, 1:2]), rd=[], wr=[rbl_b])
        def mk_yinit(e):
            return [e.dma_start(out=y_fm[oc * 128:(oc + 1) * 128, :],
                                in_=x_fm[oc * 128:(oc + 1) * 128, :])
                    for oc in range(8)]
        p.op("sync", mk_yinit, wr=y_b, dma=True, n=8)

        def ln(src_r, src_b, dst_fn, dst_b, dst_first, dst_last, gain=None,
               bf=False, router=False):
            mu = st_s[0:1, 0:512]
            tm = st_s[0:1, 512:1024]
            oc_col = ones_cb if bf else ones_c

            def mk_sum(e):
                for fc in range(8):
                    mm = e.matmul(ps_t[0][0:1, :], oc_col[:], src_r(fc),
                                  start=(fc == 0), stop=(fc == 7))
                return mm
            p.op("tensor", mk_sum, rd=[src_b, ones_c_b], wr=[ps_b[0]])
            scr = (scrB_s, scrC_s)
            scrb = (scrB_b, scrC_b)
            for fc in range(8):
                p.op("scalar", lambda e, fc=fc: e.activation(
                    scr[fc % 2][:], src_r(fc), AF.Square),
                    rd=[src_b], wr=[scrb[fc % 2]])
                p.op("tensor", lambda e, fc=fc: e.matmul(
                    ps_t[1][0:1, :], ones_c[:], scr[fc % 2][:],
                    start=(fc == 0), stop=(fc == 7)),
                    rd=[scrb[fc % 2], ones_c_b],
                    wr=[ps_b[1]] if fc == 7 else [],
                    warf=[ps_b[1]] if fc == 0 else [])
            p.op("scalar", lambda e: e.activation(mu, ps_t[0][0:1, :], AF.Copy,
                                                  scale=1.0 / 1024),
                 rd=[ps_b[0]], wr=[st_b])
            p.op("scalar", lambda e: e.activation(tm, mu, AF.Square),
                 rd=[st_b], wr=[st_b])
            p.op("vector", lambda e: e.scalar_tensor_tensor(
                tm, ps_t[1][0:1, :], 1.0 / 1024, tm, ALU.mult, ALU.subtract),
                rd=[ps_b[1], st_b], wr=[st_b])
            p.op("scalar", lambda e: e.activation(tm, tm, AF.Ln, bias=eps5_s[0:1, :]),
                 rd=[st_b, eps5_b], wr=[st_b])
            p.op("scalar", lambda e: e.activation(tm, tm, AF.Exp, scale=-0.5),
                 rd=[st_b], wr=[st_b])
            p.op("tensor", lambda e: e.matmul(ps_t[3][:], ones_r[:], mu,
                                              start=True, stop=True),
                 rd=[st_b, ones_r_b], wr=[ps_b[3]])
            p.op("tensor", lambda e: e.matmul(ps_t[6][:], ones_r[:], tm,
                                              start=True, stop=True),
                 rd=[st_b, ones_r_b], wr=[ps_b[6]])
            p.op("scalar", lambda e: e.activation(rstdB_s[:], ps_t[6][:], AF.Copy),
                 rd=[ps_b[6]], wr=[rstdB_b])
            for fc in range(8):
                p.op("vector", lambda e, fc=fc: e.tensor_sub(
                    scrA_s[:], src_r(fc), ps_t[3][:]),
                    rd=[src_b, ps_b[3]], wr=[scrA_b])
                wrk = [dst_b] if (dst_last and fc == 7) else []
                wfk = [dst_b] if (dst_first and fc == 0) else []
                if gain is None:
                    p.op("vector", lambda e, fc=fc: e.tensor_mul(
                        dst_fn(fc), scrA_s[:], rstdB_s[:]),
                        rd=[scrA_b, rstdB_b], wr=wrk, warf=wfk)
                else:
                    gt, gtbuf, btl, btbuf = gain
                    p.op("vector", lambda e, fc=fc: e.tensor_mul(
                        scrA_s[:], scrA_s[:], rstdB_s[:]),
                        rd=[rstdB_b], wr=[scrA_b])
                    p.op("vector", lambda e, fc=fc: e.tensor_scalar(
                        dst_fn(fc), scrA_s[:], gt[:, fc:fc + 1], btl[:, fc:fc + 1],
                        ALU.mult, ALU.add),
                        rd=[scrA_b, gtbuf, btbuf], wr=wrk, warf=wfk)
                if router:
                    p.op("vector", lambda e, fc=fc: e.tensor_mul(
                        scrC_s[:], scrA_s[:], rstdB_s[:]),
                        rd=[scrA_b, rstdB_b], wr=[scrC_b])
                    p.op("tensor", lambda e, fc=fc: e.matmul(
                        ps_t[2][0:4, :], rw32_s[:, fc * 4:(fc + 1) * 4], scrC_s[:],
                        start=(fc == 0), stop=(fc == 7)),
                        rd=[scrC_b, rw32_b],
                        wr=[ps_b[2]] if fc == 7 else [],
                        warf=[ps_b[2]] if fc == 0 else [])

        def wload(dram, half, nkc, ncols, b, col_off=0, row_off=0):
            assert nkc * ncols <= WHALF
            for k0 in range(0, nkc, NQSEM):
                k1 = min(k0 + NQSEM, nkc)
                def fn(e, k0=k0, k1=k1):
                    return [e.dma_start(
                        out=WA[:, half * WHALF + kc * ncols: half * WHALF + (kc + 1) * ncols],
                        in_=dram[row_off + kc * 128: row_off + (kc + 1) * 128,
                                 col_off:col_off + ncols])
                        for kc in range(k0, k1)]
                p.op("gpsimd", fn, wr=[b], dma=True, n=k1 - k0)

        def wv(half, kc, ncols, oc):
            base = half * WHALF + kc * ncols + oc * 128
            return WA[:, base: base + 128]

        def stage(src_fn, src_b, half, wbuf, nkc, noc, ncols, out_fn, out_b,
                  func=AF.Identity, bias=None, biasbuf=None, psum_sink=None):
            first = True
            for t in range(TL):
                for oc in range(noc):
                    pi = 4 + ((t * noc + oc) % 2)
                    def mk(e, t=t, oc=oc, pi=pi):
                        for kc in range(nkc):
                            mm = e.matmul(ps_t[pi][:], wv(half, kc, ncols, oc),
                                          src_fn(kc, t), start=(kc == 0),
                                          stop=(kc == nkc - 1))
                        return mm
                    p.op("tensor", mk, rd=[wbuf, src_b], wr=[ps_b[pi]])
                    if psum_sink is not None:
                        psum_sink(oc, t, pi)
                    else:
                        kw = {}
                        rdl = [ps_b[pi]]
                        if bias is not None:
                            kw["bias"] = bias[:, oc:oc + 1]
                            rdl.append(biasbuf)
                        last = (t == TL - 1 and oc == noc - 1)
                        p.op("scalar", lambda e, t=t, oc=oc, pi=pi, kw=kw: e.activation(
                            out_fn(oc, t), ps_t[pi][:], func, **kw),
                            rd=rdl, wr=[out_b] if last else [],
                            warf=[out_b] if first else [])
                    first = False

        def bcast_gate(eidx, t):
            gt_s = gfA_s if eidx < 2 else gfB_s
            base = 32 * (eidx % 2)
            row = gt_s[base:base + 1, t * 512:(t + 1) * 512]
            lo = ones64[base:base + 1, :]
            p.op("tensor", lambda e, row=row, lo=lo: e.matmul(
                ps_t[2][:], lo, row,
                start=True, stop=True), rd=[gf_b, ones64_b], wr=[ps_b[2]])
            p.op("scalar", lambda e: e.activation(gbt_s[:], ps_t[2][:], AF.Copy),
                 rd=[ps_b[2]], wr=[gbt_b])

        def sink_gated(eidx, bias=None, biasbuf=None):
            def sink(oc, t, pi):
                if oc == 0:
                    bcast_gate(eidx, t)
                bi = (oc + t) % 2
                tb = tmp_b[bi]
                tv = tmp_s[:, bi * 512:(bi + 1) * 512]
                if bias is not None:
                    p.op("scalar", lambda e, oc=oc, pi=pi: e.activation(
                        scrA_s[:], ps_t[pi][:], AF.Identity, bias=bias[:, oc:oc + 1]),
                        rd=[ps_b[pi], biasbuf], wr=[scrA_b])
                    p.op("vector", lambda e: e.tensor_mul(tv, scrA_s[:], gbt_s[:]),
                         rd=[scrA_b, gbt_b], wr=[tb])
                else:
                    p.op("vector", lambda e, pi=pi: e.tensor_mul(
                        tv, ps_t[pi][:], gbt_s[:]), rd=[ps_b[pi], gbt_b], wr=[tb])
                p.op("gpsimd", lambda e, oc=oc, t=t: e.dma_start(
                    out=y_fm[oc * 128:(oc + 1) * 128, t * 512:(t + 1) * 512],
                    in_=tv, accum_op=ALU.add),
                    rd=[tb], wr=[y_b[oc * TL + t]], dma=True)
            return sink

        def sink_addinto(dst_fn, dst_b, bias=None, biasbuf=None):
            def sink(oc, t, pi):
                if bias is not None:
                    p.op("scalar", lambda e, oc=oc, pi=pi: e.activation(
                        scrA_s[:], ps_t[pi][:], AF.Identity, bias=bias[:, oc:oc + 1]),
                        rd=[ps_b[pi], biasbuf], wr=[scrA_b])
                    p.op("vector", lambda e, oc=oc, t=t: e.tensor_add(
                        dst_fn(oc, t), dst_fn(oc, t), scrA_s[:]),
                        rd=[scrA_b], wr=[dst_b])
                else:
                    p.op("vector", lambda e, oc=oc, t=t, pi=pi: e.tensor_add(
                        dst_fn(oc, t), dst_fn(oc, t), ps_t[pi][:]),
                        rd=[ps_b[pi]], wr=[dst_b])
            return sink

        def mk_rwld(e):
            return [e.dma_start(out=rw32_s[:, kc * 4:(kc + 1) * 4],
                                in_=wd["router_w"][kc * 128:(kc + 1) * 128, :])
                    for kc in range(8)]
        p.op("sync", mk_rwld, wr=[rw32_b], dma=True, n=8)
        for t in range(TL):
            def mk_xld(e, t=t):
                return [e.dma_start(out=xt_r[:, fc, :],
                                    in_=x_fm[fc * 128:(fc + 1) * 128,
                                             t * 512:(t + 1) * 512])
                        for fc in range(8)]
            p.op("sync", mk_xld, wr=[xt_b], dma=True, n=8)
            ln(lambda fc: xt_r[:, fc, :], xt_b,
               lambda fc, t=t: xl_r[:, fc, t * 512:(t + 1) * 512], xl_b,
               t == 0, t == TL - 1, router=True)

            p.op("scalar", lambda e: e.activation(scrA_s[0:4, :], ps_t[2][0:4, :],
                                                  AF.Identity, bias=rbl_s[:, 0:1]),
                 rd=[ps_b[2], rbl_b], wr=[scrA_b])
            for tb2 in range(4):
                p.op("tensor", lambda e, tb2=tb2: e.matmul(
                    ps_t[2][:, 0:4], scrA_s[0:4, tb2 * 128:(tb2 + 1) * 128],
                    id4_s[:], start=True, stop=True),
                    rd=[scrA_b, id4_b], wr=[ps_b[2]])
                p.op("scalar", lambda e: e.activation(lt_s[:, 0:4], ps_t[2][:, 0:4],
                                                      AF.Copy), rd=[ps_b[2]], wr=[lt_b])
                v = lt_s[:, 0:4]
                m1 = lt_s[:, 4:5]
                nm1 = lt_s[:, 5:6]
                m2 = lt_s[:, 6:7]
                msk = lt_s[:, 8:12]
                v2 = lt_s[:, 12:16]
                ex = lt_s[:, 16:20]
                den = lt_s[:, 20:21]
                p.op("vector", lambda e: e.tensor_reduce(m1, v, AX.X, ALU.max),
                     wr=[lt_b])
                p.op("vector", lambda e: e.tensor_scalar(msk, v, m1, None, ALU.is_ge),
                     wr=[lt_b])
                p.op("vector", lambda e: e.scalar_tensor_tensor(
                    v2, msk, -1e30, v, ALU.mult, ALU.add), wr=[lt_b])
                p.op("vector", lambda e: e.tensor_reduce(m2, v2, AX.X, ALU.max),
                     wr=[lt_b])
                p.op("vector", lambda e: e.tensor_scalar(msk, v, m2, None, ALU.is_ge),
                     wr=[lt_b])
                p.op("vector", lambda e: e.tensor_scalar(nm1, m1, -1.0, None, ALU.mult),
                     wr=[lt_b])
                p.op("scalar", lambda e: e.activation(ex, v, AF.Exp, bias=nm1),
                     rd=[], wr=[lt_b])
                p.op("vector", lambda e: e.tensor_mul(ex, ex, msk), wr=[lt_b])
                p.op("vector", lambda e: e.tensor_reduce(den, ex, AX.X, ALU.add),
                     wr=[lt_b])
                p.op("vector", lambda e: e.reciprocal(den, den), wr=[lt_b])
                p.op("vector", lambda e: e.tensor_scalar(ex, ex, den, None, ALU.mult),
                     wr=[lt_b])
                gtm_r = gtm_s.rearrange("p (a b) -> p a b", b=32)
                p.op("vector", lambda e: e.tensor_copy(
                    gtm_r[:, :, 0:1], ex.rearrange("p (a b) -> p a b", b=1)),
                    rd=[lt_b], wr=[gtm_b])
                p.op("tensor", lambda e: e.matmul(ps_t[2][:, 0:128], gtm_s[:],
                                                  idf_s[:], start=True, stop=True),
                     rd=[gtm_b, idf_b], wr=[ps_b[2]])
                p.op("scalar", lambda e, t=t, tb2=tb2: e.activation(
                    gfA_s[0:64, t * 512 + tb2 * 128: t * 512 + (tb2 + 1) * 128],
                    ps_t[2][0:64, 0:128], AF.Copy), rd=[ps_b[2]],
                    warf=[gf_b] if (t == 0 and tb2 == 0) else [])
                p.op("scalar", lambda e, t=t, tb2=tb2: e.activation(
                    gfB_s[0:64, t * 512 + tb2 * 128: t * 512 + (tb2 + 1) * 128],
                    ps_t[2][64:128, 0:128], AF.Copy), rd=[ps_b[2]],
                    wr=[gf_b] if (t == TL - 1 and tb2 == 3) else [])

        for _b in (tmp_b[0], tmp_b[1], gbt_b):
            if xt_b.w is not None:
                _b.r.append(("x", xt_b.w))
            _b.r.extend(xt_b.r)
        xl_fn = lambda kc, t: xl_r[:, kc, t * 512:(t + 1) * 512]
        for half in range(2):
            w1b = rebind("w1h", [WAb[0]]); WAb[0] = w1b
            wload(wd["sw_w1"], 0, 8, 2048, w1b, col_off=half * 2048)
            Ab = rebind("A", SAb); SAb = [Ab, Ab]
            stage(xl_fn, xl_b, 0, w1b, 8, 16, 2048, sten2, Ab, func=AF.Silu)
            w3b = rebind("w3h", [WAb[1]]); WAb[1] = w3b
            wload(wd["sw_w3"], 1, 8, 2048, w3b, col_off=half * 2048)
            def sink_ab(oc, t, pi, Ab=Ab):
                p.op("vector", lambda e, oc=oc, t=t, pi=pi: e.tensor_mul(
                    sten2(oc, t), ps_t[pi][:], sten2(oc, t)),
                    rd=[ps_b[pi]], wr=[Ab])
            stage(xl_fn, xl_b, 1, w3b, 8, 16, 2048, None, None, psum_sink=sink_ab)
            w2b = rebind("w2h", [WAb[0]]); WAb[0] = w2b
            wload(wd["sw_w2"], 0, 16, 1024, w2b, row_off=half * 2048)
            stage(sten2, Ab, 0, w2b, 16, 8, 1024, None, None,
                  psum_sink=sink_gated(0))

        for half in range(2):
            w1b = rebind("mw1", [WAb[1]]); WAb[1] = w1b
            wload(wd["ml_w1"], 1, 8, 2048, w1b, col_off=half * 2048)
            Hb = rebind("H", SAb); SAb = [Hb, Hb]
            stage(xl_fn, xl_b, 1, w1b, 8, 16, 2048, sten2, Hb, func=AF.Gelu,
                  bias=bt["ml_b1"][:, half * 16:(half + 1) * 16],
                  biasbuf=btb["ml_b1"])
            w2b = rebind("mw2", [WAb[0]]); WAb[0] = w2b
            wload(wd["ml_w2"], 0, 16, 1024, w2b, row_off=half * 2048)
            stage(sten2, Hb, 0, w2b, 16, 8, 1024, None, None,
                  psum_sink=sink_gated(3, bias=(bt["ml_b2"] if half == 1 else None),
                                       biasbuf=btb["ml_b2"]))

        eq_b = rebind("eq", SAb); SAb = [eq_b, SAb[1]]
        w = rebind("weq", [WAb[1]]); WAb[1] = w
        wload(wd["me_eq_w"], 1, 8, 1024, w)
        stage(xl_fn, xl_b, 1, w, 8, 8, 1024, sten(0), eq_b,
              bias=bt["me_eq_b"], biasbuf=btb["me_eq_b"])
        tv_b = rebind("tv", [SAb[1]]); SAb = [eq_b, tv_b]
        w = rebind("wwv", [WAb[0]]); WAb[0] = w
        wload(wd["me_wv"], 0, 8, 1024, w)
        stage(sten(0), eq_b, 0, w, 8, 8, 1024, sten(1), tv_b,
              bias=bt["me_bv"], biasbuf=btb["me_bv"])
        sym_b = rebind("sym", [eq_b]); SAb = [sym_b, tv_b]
        w = rebind("wwo", [WAb[1]]); WAb[1] = w
        wload(wd["me_wo"], 1, 8, 1024, w)
        stage(sten(1), tv_b, 1, w, 8, 8, 1024, sten(0), sym_b,
              bias=bt["me_bo"], biasbuf=btb["me_bo"])
        for half in range(2):
            c1b = rebind("c1", [SAb[1]]); SAb = [sym_b, c1b]
            w = rebind("wc1", [WAb[0]]); WAb[0] = w
            wload(wd["me_c1w"], 0, 8, 1024, w, col_off=half * 1024)
            stage(sten(0), sym_b, 0, w, 8, 8, 1024, sten(1), c1b, func=AF.Gelu,
                  bias=bt["me_c1b"][:, half * 8:(half + 1) * 8],
                  biasbuf=btb["me_c1b"])
            w = rebind("wc2", [WAb[1]]); WAb[1] = w
            wload(wd["me_c2w"], 1, 8, 1024, w, row_off=half * 1024)
            stage(sten(1), c1b, 1, w, 8, 8, 1024, None, None,
                  psum_sink=sink_gated(1, bias=(bt["me_c2b"] if half == 1 else None),
                                       biasbuf=btb["me_c2b"]))

        syn_b = rebind("syn", [SAb[0]]); SAb = [syn_b, SAb[1]]
        w = rebind("wsyn", [WAb[0]]); WAb[0] = w
        wload(wd["ce_syn_w"], 0, 8, 1024, w)
        if stop_after == 'wsyn':
            p.build(); return nc
        stage(xl_fn, xl_b, 0, w, 8, 8, 1024, sten(0), syn_b,
              bias=bt["ce_syn_b"], biasbuf=btb["ce_syn_b"])
        t2_b = rebind("t2", [SAb[1]]); SAb = [syn_b, t2_b]
        w = rebind("wcv", [WAb[1]]); WAb[1] = w
        wload(wd["ce_wv"], 1, 8, 1024, w)
        stage(sten(0), syn_b, 1, w, 8, 8, 1024, sten(1), t2_b,
              bias=bt["ce_bv"], biasbuf=btb["ce_bv"])
        w = rebind("wco", [WAb[0]]); WAb[0] = w
        wload(wd["ce_wo"], 0, 8, 1024, w)
        stage(sten(1), t2_b, 0, w, 8, 8, 1024, None, None,
              psum_sink=sink_addinto(lambda oc, t: sten(0)(oc, t), syn_b,
                                     bt["ce_bo"], btb["ce_bo"]))
        if stop_after == 's':
            p.build(); return nc
        h2_b = rebind("h2", [t2_b]); SAb = [syn_b, h2_b]
        for t in range(TL):
            ln(lambda fc, t=t: sten(0)(fc, t), syn_b,
               lambda fc, t=t: sten(1)(fc, t), h2_b, t == 0, t == TL - 1,
               gain=(bt["ce_n1g"], btb["ce_n1g"], bt["ce_n1b"], btb["ce_n1b"]),
               bf=True)
        xlf_b = rebind("ffacc", [xl_b])
        def xlf(oc, t):
            return xl_r[:, oc, t * 512:(t + 1) * 512]
        for half in range(2):
            f1b = rebind("f1", [syn_b] if half == 0 else [SAb[0]])
            SAb = [f1b, h2_b]
            w = rebind("wf1", [WAb[1]]); WAb[1] = w
            wload(wd["ce_f1w"], 1, 8, 1024, w, col_off=half * 1024)
            stage(sten(1), h2_b, 1, w, 8, 8, 1024, sten(0), f1b, func=AF.Relu,
                  bias=bt["ce_f1b"][:, half * 8:(half + 1) * 8],
                  biasbuf=btb["ce_f1b"])
            w = rebind("wf2", [WAb[0]]); WAb[0] = w
            wload(wd["ce_f2w"], 0, 8, 1024, w, row_off=half * 1024)
            if half == 0:
                def sink0(oc, t, pi):
                    p.op("scalar", lambda e, oc=oc, t=t, pi=pi: e.activation(
                        xlf(oc, t), ps_t[pi][:], AF.Identity,
                        bias=bt["ce_f2b"][:, oc:oc + 1]),
                        rd=[ps_b[pi], btb["ce_f2b"]], wr=[xlf_b])
                stage(sten(0), f1b, 0, w, 8, 8, 1024, None, None, psum_sink=sink0)
            else:
                def sink1(oc, t, pi):
                    p.op("vector", lambda e, oc=oc, t=t, pi=pi: e.tensor_add(
                        xlf(oc, t), xlf(oc, t), ps_t[pi][:]),
                        rd=[ps_b[pi]], wr=[xlf_b])
                stage(sten(0), f1b, 0, w, 8, 8, 1024, None, None, psum_sink=sink1)
        s2_b = rebind("s2", [SAb[0]])
        for t in range(TL):
            for fc in range(8):
                p.op("vector", lambda e, t=t, fc=fc: e.tensor_add(
                    sten(0)(fc, t), sten(1)(fc, t), xlf(fc, t)),
                    rd=[h2_b, xlf_b],
                    wr=[s2_b] if (t == TL - 1 and fc == 7) else [],
                    warf=[s2_b] if (t == 0 and fc == 0) else [])
        h2b_b = rebind("h2b", [h2_b]); SAb = [s2_b, h2b_b]
        for t in range(TL):
            ln(lambda fc, t=t: sten(0)(fc, t), s2_b,
               lambda fc, t=t: sten(1)(fc, t), h2b_b, t == 0, t == TL - 1,
               gain=(bt["ce_n2g"], btb["ce_n2g"], bt["ce_n2b"], btb["ce_n2b"]),
               bf=True)
        w = rebind("wgen", [WAb[1]]); WAb[1] = w
        wload(wd["ce_gen_w"], 1, 8, 1024, w)
        stage(sten(1), h2b_b, 1, w, 8, 8, 1024, None, None,
              psum_sink=sink_gated(2, bias=bt["ce_gen_b"], biasbuf=btb["ce_gen_b"]))

        p.build()
    return nc


def prep_l0_ffn_inmaps(xown_fm_list, weights, NT):
    g = lambda n: np.asarray(weights[n], np.float32)
    def btile(v, nch):
        return np.ascontiguousarray(np.asarray(v, np.float32).reshape(nch, 128).T)
    common = dict(
        router_w=g("router_w"), sw_w1=g("sw_w1"), sw_w3=g("sw_w3"), sw_w2=g("sw_w2"),
        me_eq_w=g("me_eq_w"), me_wv=g("me_wv"), me_wo=g("me_wo"),
        me_c1w=g("me_c1w"), me_c2w=g("me_c2w"), ce_syn_w=g("ce_syn_w"),
        ce_wv=g("ce_wv"), ce_wo=g("ce_wo"), ce_f1w=g("ce_f1w"), ce_f2w=g("ce_f2w"),
        ce_gen_w=g("ce_gen_w"), ml_w1=g("ml_w1"), ml_w2=g("ml_w2"),
        me_eq_b=btile(g("me_eq_b"), 8), me_bv=btile(g("me_bv"), 8),
        me_bo=btile(g("me_bo"), 8), me_c1b=btile(g("me_c1b"), 16),
        me_c2b=btile(g("me_c2b"), 8), ce_syn_b=btile(g("ce_syn_b"), 8),
        ce_bv=btile(g("ce_bv"), 8), ce_bo=btile(g("ce_bo"), 8),
        ce_n1g=btile(g("ce_n1g"), 8), ce_n1b=btile(g("ce_n1b"), 8),
        ce_f1b=btile(g("ce_f1b"), 16), ce_f2b=btile(g("ce_f2b"), 8),
        ce_n2g=btile(g("ce_n2g"), 8), ce_n2b=btile(g("ce_n2b"), 8),
        ce_gen_b=btile(g("ce_gen_b"), 8), ml_b1=btile(g("ml_b1"), 32),
        ml_b2=btile(g("ml_b2"), 8),
        router_b=np.ascontiguousarray(g("router_b").reshape(4, 1)),
        load_balancer=np.ascontiguousarray(g("load_balancer").reshape(4, 1)),
        idmat=np.eye(128, dtype=np.float32),
    )
    return [dict(common, x_fm=x) for x in xown_fm_list]



def build_l1_ffn(nc, NT):
    D, E16, F4 = 1024, 16, 4096
    TL = NT // 512

    x_fm = nc.dram_tensor("x_fm", [D, NT], F32, kind="ExternalInput").ap()
    y_fm = nc.dram_tensor("y_fm", [D, NT], F32, kind="ExternalOutput").ap()
    w1t_d = nc.dram_tensor("w1t", [E16, D, F4], F32, kind="ExternalInput").ap()
    w2t_d = nc.dram_tensor("w2t", [E16, F4, D], F32, kind="ExternalInput").ap()
    b1t_d = nc.dram_tensor("b1t", [128, E16 * 32], F32, kind="ExternalInput").ap()
    b2t_d = nc.dram_tensor("b2t", [128, E16 * 8], F32, kind="ExternalInput").ap()
    keyt_d = nc.dram_tensor("keyt", [D, E16], F32, kind="ExternalInput").ap()
    rwt_d = nc.dram_tensor("rwt", [D, E16], F32, kind="ExternalInput").ap()
    rb_d = nc.dram_tensor("rb", [1, E16], F32, kind="ExternalInput").ap()
    id16_d = nc.dram_tensor("id16", [16, 16], F32, kind="ExternalInput").ap()
    idm_d = nc.dram_tensor("idmat", [128, 128], F32, kind="ExternalInput").ap()

    with contextlib.ExitStack() as es:
        p = Prog(nc)
        E = es.enter_context

        xl_s = E(nc.sbuf_tensor("xl_s", [128, 8 * NT], BF16))
        H_s = E(nc.sbuf_tensor("H_s", [128, 32 * 512], BF16))
        W1A = E(nc.sbuf_tensor("W1A", [128, 2 * 4096], BF16))
        W2A = E(nc.sbuf_tensor("W2A", [128, 2 * 4096], BF16))
        xt_s = E(nc.sbuf_tensor("xt_s", [128, 4096], F32))
        rwk_s = E(nc.sbuf_tensor("rwk_s", [128, 2 * 8 * E16], F32))
        gtm_s = E(nc.sbuf_tensor("gtm_s", [128, (NT // 128) * 16], F32))
        grow_s = E(nc.sbuf_tensor("grow_s", [1, 512], F32))
        gbt_s = E(nc.sbuf_tensor("gbt_s", [128, 512], F32))
        tmp_s = E(nc.sbuf_tensor("tmp_s", [128, 2 * 512], F32))
        scrA_s = E(nc.sbuf_tensor("scrA_s", [128, 512], F32))
        scrB_s = E(nc.sbuf_tensor("scrB_s", [128, 512], F32))
        scrC_s = E(nc.sbuf_tensor("scrC_s", [128, 512], F32))
        st_s = E(nc.sbuf_tensor("st_s", [1, 1024], F32))
        rstdB_s = E(nc.sbuf_tensor("rstdB_s", [128, 512], F32))
        lt_s = E(nc.sbuf_tensor("lt_s", [128, 80], F32))
        b1t_s = E(nc.sbuf_tensor("b1t_s", [128, E16 * 32], F32))
        b2t_s = E(nc.sbuf_tensor("b2t_s", [128, E16 * 8], F32))
        rbk_s = E(nc.sbuf_tensor("rbk_s", [1, 2 * E16], F32))
        id16_s = E(nc.sbuf_tensor("id16_s", [16, 16], F32))
        idf_s = E(nc.sbuf_tensor("idf_s", [128, 128], F32))
        ones_c = E(nc.sbuf_tensor("ones_c", [128, 1], F32))
        ones_r = E(nc.sbuf_tensor("ones_r", [1, 128], F32))
        eps5_s = E(nc.sbuf_tensor("eps5_s", [128, 1], F32))

        ps_t, ps_b = [], []
        for i in range(7):
            ps_t.append(E(nc.psum_tensor(f"ps{i}", [128, 512], F32)))
            ps_b.append(Buf(f"ps{i}", psum=True))
        ps_t.append(E(nc.psum_tensor("ps7", [128, 512], F32)))
        ps_b.append(Buf("ps7", psum=True))

        xl_b, xt_b, H_b = Buf("xl"), Buf("xt"), Buf("H")
        W1b = [Buf("W1a"), Buf("W1b")]
        W2b = [Buf("W2a"), Buf("W2b")]
        rwk_b, gtm_b, grow_b, gbt_b = Buf("rwk"), Buf("gtm"), Buf("grow"), Buf("gbt")
        tmp_b = [Buf("tmp0"), Buf("tmp1")]
        scrA_b, scrB_b, scrC_b = Buf("scrA"), Buf("scrB"), Buf("scrC")
        st_b, rstdB_b, lt_b = Buf("st"), Buf("rstdB"), Buf("lt")
        b1t_b, b2t_b, rbk_b = Buf("b1t"), Buf("b2t"), Buf("rbk")
        id16_b, idf_b = Buf("id16"), Buf("idf")
        ones_c_b, ones_r_b, eps5_b = Buf("ones_c"), Buf("ones_r"), Buf("eps5")
        y_b = [Buf(f"y{i}") for i in range(8 * TL)]

        xl_r = xl_s.rearrange("p (f t) -> p f t", f=8)
        xt_r = xt_s.rearrange("p (f t) -> p f t", f=8)
        H_r = H_s.rearrange("p (f t) -> p f t", f=32)

        p.op("sync", lambda e: e.dma_start(out=idf_s[:], in_=idm_d), wr=[idf_b], dma=True)
        p.op("sync", lambda e: e.dma_start(out=id16_s[:], in_=id16_d), wr=[id16_b], dma=True)
        p.op("sync", lambda e: e.dma_start(out=b1t_s[:], in_=b1t_d), wr=[b1t_b], dma=True)
        p.op("sync", lambda e: e.dma_start(out=b2t_s[:], in_=b2t_d), wr=[b2t_b], dma=True)
        p.op("sync", lambda e: e.dma_start(out=rbk_s[0:1, 0:E16], in_=rb_d),
             wr=[rbk_b], dma=True)

        def mk_kld(e):
            return [e.dma_start(out=rwk_s[:, kc * E16:(kc + 1) * E16],
                                in_=keyt_d[kc * 128:(kc + 1) * 128, :])
                    for kc in range(8)]
        p.op("sync", mk_kld, wr=[rwk_b], dma=True, n=8)

        def mk_rld(e):
            return [e.dma_start(out=rwk_s[:, (8 + kc) * E16:(9 + kc) * E16],
                                in_=rwt_d[kc * 128:(kc + 1) * 128, :])
                    for kc in range(8)]
        p.op("sync", mk_rld, wr=[rwk_b], dma=True, n=8)
        p.op("vector", lambda e: e.memset(ones_c[:], 1.0), wr=[ones_c_b])
        p.op("vector", lambda e: e.memset(ones_r[:], 1.0), wr=[ones_r_b])
        p.op("vector", lambda e: e.memset(eps5_s[:], 1e-5), wr=[eps5_b])

        def mk_yinit(e):
            return [e.dma_start(out=y_fm[oc * 128:(oc + 1) * 128, :],
                                in_=x_fm[oc * 128:(oc + 1) * 128, :])
                    for oc in range(8)]
        p.op("sync", mk_yinit, wr=y_b, dma=True, n=8)

        scr = (scrB_s, scrC_s)
        scrb = (scrB_b, scrC_b)
        for kc in range(8):
            p.op("scalar", lambda e, kc=kc: e.activation(
                scr[kc % 2][:, 0:E16], rwk_s[:, kc * E16:(kc + 1) * E16], AF.Square),
                rd=[rwk_b], wr=[scrb[kc % 2]])
            p.op("tensor", lambda e, kc=kc: e.matmul(
                ps_t[0][0:1, 0:E16], ones_c[:], scr[kc % 2][:, 0:E16],
                start=(kc == 0), stop=(kc == 7)),
                rd=[scrb[kc % 2], ones_c_b],
                wr=[ps_b[0]] if kc == 7 else [], warf=[ps_b[0]] if kc == 0 else [])
        p.op("scalar", lambda e: e.activation(rbk_s[0:1, E16:2 * E16],
                                              ps_t[0][0:1, 0:E16], AF.Copy),
             rd=[ps_b[0]], wr=[rbk_b])
        p.op("vector", lambda e: e.tensor_sub(rbk_s[0:1, E16:2 * E16],
                                              rbk_s[0:1, 0:E16],
                                              rbk_s[0:1, E16:2 * E16]),
             rd=[], wr=[rbk_b])
        for kc in range(8):
            p.op("vector", lambda e, kc=kc: e.scalar_tensor_tensor(
                rwk_s[:, kc * E16:(kc + 1) * E16],
                rwk_s[:, kc * E16:(kc + 1) * E16], 2.0,
                rwk_s[:, (8 + kc) * E16:(9 + kc) * E16], ALU.mult, ALU.add),
                rd=[], wr=[rwk_b])
        p.op("tensor", lambda e: e.matmul(ps_t[0][0:16, 0:1],
                                          rbk_s[0:1, E16:2 * E16],
                                          ones_r[0:1, 0:1], start=True, stop=True),
             rd=[rbk_b, ones_r_b], wr=[ps_b[0]])
        p.op("scalar", lambda e: e.activation(lt_s[0:16, 0:1], ps_t[0][0:16, 0:1],
                                              AF.Copy), rd=[ps_b[0]], wr=[lt_b])

        def ln(src_r, src_b, dst_fn, dst_b, dfirst, dlast):
            mu = st_s[0:1, 0:512]
            tm = st_s[0:1, 512:1024]

            def mk_sum(e):
                for fc in range(8):
                    mm = e.matmul(ps_t[0][0:1, :], ones_c[:], src_r(fc),
                                  start=(fc == 0), stop=(fc == 7))
                return mm
            p.op("tensor", mk_sum, rd=[src_b, ones_c_b], wr=[ps_b[0]])
            for fc in range(8):
                p.op("scalar", lambda e, fc=fc: e.activation(
                    scr[fc % 2][:], src_r(fc), AF.Square),
                    rd=[src_b], wr=[scrb[fc % 2]])
                p.op("tensor", lambda e, fc=fc: e.matmul(
                    ps_t[1][0:1, :], ones_c[:], scr[fc % 2][:],
                    start=(fc == 0), stop=(fc == 7)),
                    rd=[scrb[fc % 2], ones_c_b],
                    wr=[ps_b[1]] if fc == 7 else [],
                    warf=[ps_b[1]] if fc == 0 else [])
            p.op("scalar", lambda e: e.activation(mu, ps_t[0][0:1, :], AF.Copy,
                                                  scale=1.0 / 1024),
                 rd=[ps_b[0]], wr=[st_b])
            p.op("scalar", lambda e: e.activation(tm, mu, AF.Square),
                 rd=[st_b], wr=[st_b])
            p.op("vector", lambda e: e.scalar_tensor_tensor(
                tm, ps_t[1][0:1, :], 1.0 / 1024, tm, ALU.mult, ALU.subtract),
                rd=[ps_b[1], st_b], wr=[st_b])
            p.op("scalar", lambda e: e.activation(tm, tm, AF.Ln, bias=eps5_s[0:1, :]),
                 rd=[st_b, eps5_b], wr=[st_b])
            p.op("scalar", lambda e: e.activation(tm, tm, AF.Exp, scale=-0.5),
                 rd=[st_b], wr=[st_b])
            p.op("tensor", lambda e: e.matmul(ps_t[3][:], ones_r[:], mu,
                                              start=True, stop=True),
                 rd=[st_b, ones_r_b], wr=[ps_b[3]])
            p.op("tensor", lambda e: e.matmul(ps_t[6][:], ones_r[:], tm,
                                              start=True, stop=True),
                 rd=[st_b, ones_r_b], wr=[ps_b[6]])
            p.op("scalar", lambda e: e.activation(rstdB_s[:], ps_t[6][:], AF.Copy),
                 rd=[ps_b[6]], wr=[rstdB_b])
            for fc in range(8):
                p.op("vector", lambda e, fc=fc: e.tensor_sub(
                    scrA_s[:], src_r(fc), ps_t[3][:]),
                    rd=[src_b, ps_b[3]], wr=[scrA_b])
                p.op("vector", lambda e, fc=fc: e.tensor_mul(
                    dst_fn(fc), scrA_s[:], rstdB_s[:]),
                    rd=[scrA_b, rstdB_b],
                    wr=[dst_b] if (dlast and fc == 7) else [],
                    warf=[dst_b] if (dfirst and fc == 0) else [])
                p.op("vector", lambda e, fc=fc: e.tensor_mul(
                    scrC_s[:], scrA_s[:], rstdB_s[:]),
                    rd=[scrA_b, rstdB_b], wr=[scrC_b])
                p.op("tensor", lambda e, fc=fc: e.matmul(
                    ps_t[2][0:16, :], rwk_s[:, fc * E16:(fc + 1) * E16], scrC_s[:],
                    start=(fc == 0), stop=(fc == 7)),
                    rd=[scrC_b, rwk_b],
                    wr=[ps_b[2]] if fc == 7 else [],
                    warf=[ps_b[2]] if fc == 0 else [])

        for t in range(TL):
            def mk_xld(e, t=t):
                return [e.dma_start(out=xt_r[:, fc, :],
                                    in_=x_fm[fc * 128:(fc + 1) * 128,
                                             t * 512:(t + 1) * 512])
                        for fc in range(8)]
            p.op("sync", mk_xld, wr=[xt_b], dma=True, n=8)
            ln(lambda fc: xt_r[:, fc, :], xt_b,
               lambda fc, t=t: xl_r[:, fc, t * 512:(t + 1) * 512], xl_b,
               t == 0, t == TL - 1)
            p.op("scalar", lambda e: e.activation(scrA_s[0:16, :], ps_t[2][0:16, :],
                                                  AF.Identity, bias=lt_s[0:16, 0:1]),
                 rd=[ps_b[2], lt_b], wr=[scrA_b])
            for tb in range(4):
                p.op("tensor", lambda e, tb=tb: e.matmul(
                    ps_t[2][:, 0:16], scrA_s[0:16, tb * 128:(tb + 1) * 128],
                    id16_s[:], start=True, stop=True),
                    rd=[scrA_b, id16_b], wr=[ps_b[2]])
                v = lt_s[:, 16:32]
                mx = lt_s[:, 32:33]
                nmx = lt_s[:, 33:34]
                m4 = lt_s[:, 34:35]
                msk = lt_s[:, 64:80]
                p.op("scalar", lambda e: e.activation(v, ps_t[2][:, 0:16], AF.Copy),
                     rd=[ps_b[2]], wr=[lt_b])
                p.op("vector", lambda e: e.tensor_reduce(mx, v, AX.X, ALU.max),
                     wr=[lt_b])
                p.op("vector", lambda e: e.tensor_copy(lt_s[:, 35:36], mx), wr=[lt_b])
                for it in range(3):
                    p.op("vector", lambda e: e.tensor_scalar(
                        msk, v, lt_s[:, 35:36], None, ALU.is_ge), wr=[lt_b])
                    p.op("vector", lambda e: e.scalar_tensor_tensor(
                        lt_s[:, 36:52], msk, -1e30, v, ALU.mult, ALU.add),
                        wr=[lt_b])
                    p.op("vector", lambda e: e.tensor_reduce(
                        lt_s[:, 35:36], lt_s[:, 36:52], AX.X, ALU.max), wr=[lt_b])
                p.op("vector", lambda e: e.tensor_scalar(
                    msk, v, lt_s[:, 35:36], None, ALU.is_ge), wr=[lt_b])
                p.op("vector", lambda e: e.tensor_scalar(
                    nmx, mx, -1.0, None, ALU.mult), wr=[lt_b])
                p.op("scalar", lambda e: e.activation(lt_s[:, 36:52], v, AF.Exp,
                                                      bias=nmx), rd=[], wr=[lt_b])
                p.op("vector", lambda e: e.tensor_mul(lt_s[:, 36:52],
                                                      lt_s[:, 36:52], msk), wr=[lt_b])
                p.op("vector", lambda e: e.tensor_reduce(
                    m4, lt_s[:, 36:52], AX.X, ALU.add), wr=[lt_b])
                p.op("vector", lambda e: e.reciprocal(m4, m4), wr=[lt_b])
                p.op("vector", lambda e, t=t, tb=tb: e.tensor_scalar(
                    gtm_s[:, (t * 4 + tb) * 16:(t * 4 + tb + 1) * 16],
                    lt_s[:, 36:52], m4, None, ALU.mult),
                    rd=[lt_b],
                    wr=[gtm_b] if (t == TL - 1 and tb == 3) else [],
                    warf=[gtm_b] if (t == 0 and tb == 0) else [])

        def mk_gbt(eidx, t):
            def mk1(e, eidx=eidx, t=t):
                for tb in range(4):
                    mm = e.matmul(ps_t[2][0:1, tb * 128:(tb + 1) * 128],
                                  gtm_s[:, (t * 4 + tb) * 16 + eidx:
                                        (t * 4 + tb) * 16 + eidx + 1],
                                  idf_s[:], start=True, stop=True)
                return mm
            p.op("tensor", mk1, rd=[gtm_b, idf_b], wr=[ps_b[2]])
            p.op("scalar", lambda e: e.activation(grow_s[:], ps_t[2][0:1, :],
                                                  AF.Copy), rd=[ps_b[2]], wr=[grow_b])
            p.op("tensor", lambda e: e.matmul(ps_t[2][:], ones_r[:], grow_s[:],
                                              start=True, stop=True),
                 rd=[grow_b, ones_r_b], wr=[ps_b[2]])
            p.op("scalar", lambda e: e.activation(gbt_s[:], ps_t[2][:], AF.Copy),
                 rd=[ps_b[2]], wr=[gbt_b])

        for t in range(TL):
            for eidx in range(E16):
                Hb = rebind("H", [H_b])
                globals_H = Hb
                for fs in range(8):
                    wb = rebind("W1", [W1b[fs % 2]])
                    W1b[fs % 2] = wb

                    def mk_w1(e, eidx=eidx, fs=fs):
                        return [e.dma_start(
                            out=W1A[:, (fs % 2) * 4096 + kc * 512:
                                    (fs % 2) * 4096 + (kc + 1) * 512],
                            in_=w1t_d[eidx, kc * 128:(kc + 1) * 128,
                                      fs * 512:(fs + 1) * 512])
                            for kc in range(8)]
                    p.op("gpsimd", mk_w1, wr=[wb], dma=True, n=8)
                    for fc4 in range(4):
                        pi = 4 + (fc4 % 2)

                        def mkh(e, fs=fs, fc4=fc4, t=t, pi=pi):
                            for kc in range(8):
                                mm = e.matmul(
                                    ps_t[pi][:],
                                    W1A[:, (fs % 2) * 4096 + kc * 512 + fc4 * 128:
                                        (fs % 2) * 4096 + kc * 512 + (fc4 + 1) * 128],
                                    xl_r[:, kc, t * 512:(t + 1) * 512],
                                    start=(kc == 0), stop=(kc == 7))
                            return mm
                        p.op("tensor", mkh, rd=[wb, xl_b], wr=[ps_b[pi]])
                        fcg = fs * 4 + fc4
                        p.op("scalar", lambda e, fcg=fcg, eidx=eidx, pi=pi: e.activation(
                            H_r[:, fcg, :], ps_t[pi][:], AF.Gelu,
                            bias=b1t_s[:, eidx * 32 + fcg: eidx * 32 + fcg + 1]),
                            rd=[ps_b[pi], b1t_b],
                            wr=[Hb] if fcg == 31 else [], warf=[Hb] if fcg == 0 else [])
                H_b = Hb
                mk_gbt(eidx, t)
                for fs2 in range(8):
                    wb2 = rebind("W2", [W2b[fs2 % 2]])
                    W2b[fs2 % 2] = wb2

                    def mk_w2(e, eidx=eidx, fs2=fs2):
                        return [e.dma_start(
                            out=W2A[:, (fs2 % 2) * 4096 + fcw * 1024:
                                    (fs2 % 2) * 4096 + (fcw + 1) * 1024],
                            in_=w2t_d[eidx, fs2 * 512 + fcw * 128:
                                      fs2 * 512 + (fcw + 1) * 128, :])
                            for fcw in range(4)]
                    p.op("gpsimd", mk_w2, wr=[wb2], dma=True, n=4)
                    for dc in range(8):
                        def mko(e, fs2=fs2, dc=dc, t=t):
                            for fcw in range(4):
                                mm = e.matmul(
                                    ps_t[dc][:],
                                    W2A[:, (fs2 % 2) * 4096 + fcw * 1024 + dc * 128:
                                        (fs2 % 2) * 4096 + fcw * 1024 + (dc + 1) * 128],
                                    H_r[:, fs2 * 4 + fcw, :],
                                    start=(fs2 == 0 and fcw == 0),
                                    stop=(fs2 == 7 and fcw == 3))
                            return mm
                        p.op("tensor", mko, rd=[wb2, H_b],
                             wr=[ps_b[dc]] if fs2 == 7 else [],
                             warf=[ps_b[dc]] if fs2 == 0 else [])
                for dc in range(8):
                    bi = dc % 2
                    tb_ = tmp_b[bi]
                    tv = tmp_s[:, bi * 512:(bi + 1) * 512]
                    p.op("scalar", lambda e, dc=dc, eidx=eidx: e.activation(
                        scrA_s[:], ps_t[dc][:], AF.Identity,
                        bias=b2t_s[:, eidx * 8 + dc: eidx * 8 + dc + 1]),
                        rd=[ps_b[dc], b2t_b], wr=[scrA_b])
                    p.op("vector", lambda e, tv=tv: e.tensor_mul(
                        tv, scrA_s[:], gbt_s[:]),
                         rd=[scrA_b, gbt_b], wr=[tb_])
                    p.op("gpsimd", lambda e, dc=dc, t=t, tv=tv: e.dma_start(
                        out=y_fm[dc * 128:(dc + 1) * 128, t * 512:(t + 1) * 512],
                        in_=tv, accum_op=ALU.add),
                        rd=[tb_], wr=[y_b[dc * TL + t]], dma=True)

        p.build()
    return nc


def prep_l1_ffn_inmaps(xown_fm_list, w, NT):
    g = lambda n: np.asarray(w[n], np.float32)
    w1t = np.ascontiguousarray(g("w1").transpose(0, 2, 1))
    w2t = np.ascontiguousarray(g("w2").transpose(0, 2, 1))
    b1 = g("b1")
    b2 = g("b2")
    b1t = np.concatenate([np.ascontiguousarray(b1[e].reshape(32, 128).T)
                          for e in range(16)], axis=1)
    b2t = np.concatenate([np.ascontiguousarray(b2[e].reshape(8, 128).T)
                          for e in range(16)], axis=1)
    common = dict(
        w1t=w1t, w2t=w2t, b1t=b1t, b2t=b2t,
        keyt=np.ascontiguousarray(g("keys").T),
        rwt=np.ascontiguousarray(g("rw").T),
        rb=np.ascontiguousarray(g("rb").reshape(1, 16)),
        id16=np.eye(16, dtype=np.float32),
        idmat=np.eye(128, dtype=np.float32),
    )
    return [dict(common, x_fm=x) for x in xown_fm_list]



S_FULL = 8192


def _own_cols(S, quad):
    return np.concatenate([np.arange(t * 128, (t + 1) * 128)
                           for t in own_tile_ids(S, quad)])


def _reassemble_fm(y_list, S):
    out = [np.empty((1024, S), dtype=np.float32) for _ in range(2)]
    for c in range(8):
        b, quad = c // 4, c % 4
        y = np.asarray(y_list[c])
        for k, t in enumerate(own_tile_ids(S, quad)):
            out[b][:, t * 128:(t + 1) * 128] = y[:, k * 128:(k + 1) * 128]
    return out


def kernel(**inputs):
    S = S_FULL
    g = lambda n: np.asarray(inputs[n], dtype=np.float32)

    x = g("L0_attn_x")
    maps0 = prep_l0_attn_inmaps(x, g("L0_attn_Wq"), g("L0_attn_Wkv"),
                                g("L0_attn_Wout"), g("L0_attn_bout"), S)
    nc0 = bass.Bass("TRN2", target_bir_lowering=False)
    build_l0_attn(nc0, S)
    nc0.finalize()
    res0 = run_bass_kernel_spmd(nc0, maps0, core_ids=list(range(8)))
    y0 = [np.asarray(r["y_fm"]) for r in res0.results]

    w0f = {n[len("L0_ffn_"):]: inputs[n] for n in inputs if n.startswith("L0_ffn_")}
    maps1 = prep_l0_ffn_inmaps(y0, w0f, S // 4)
    nc1 = bass.Bass("TRN2", target_bir_lowering=False)
    build_l0_ffn(nc1, S // 4)
    nc1.finalize()
    res1 = run_bass_kernel_spmd(nc1, maps1, core_ids=list(range(8)))
    y1 = [np.asarray(r["y_fm"]) for r in res1.results]

    xb = _reassemble_fm(y1, S)
    xo = [np.ascontiguousarray(xb[c // 4][:, _own_cols(S, c % 4)]) for c in range(8)]
    maps2 = prep_l1_attn_inmaps(xb, xo, g("L1_attn_Wq"), g("L1_attn_Wk"),
                                g("L1_attn_Wv"), g("L1_attn_Wo"), S)
    nc2 = bass.Bass("TRN2", target_bir_lowering=False)
    build_l1_attn(nc2, S)
    nc2.finalize()
    res2 = run_bass_kernel_spmd(nc2, maps2, core_ids=list(range(8)))
    y2 = [np.asarray(r["y_fm"]) for r in res2.results]

    w1f = {n[len("L1_ffn_"):]: inputs[n] for n in inputs if n.startswith("L1_ffn_")}
    maps3 = prep_l1_ffn_inmaps(y2, w1f, S // 4)
    nc3 = bass.Bass("TRN2", target_bir_lowering=False)
    build_l1_ffn(nc3, S // 4)
    nc3.finalize()
    res3 = run_bass_kernel_spmd(nc3, maps3, core_ids=list(range(8)))
    y3 = [np.asarray(r["y_fm"]) for r in res3.results]

    out = np.empty((2, S, 1024), dtype=np.float32)
    for c in range(8):
        b, quad = c // 4, c % 4
        for k, t in enumerate(own_tile_ids(S, quad)):
            out[b, t * 128:(t + 1) * 128, :] = y3[c][:, k * 128:(k + 1) * 128].T
    return out
